# version 01
# baseline (speedup 1.0000x reference)
import numpy as np
from contextlib import ExitStack

import concourse.bass as bass
import concourse.mybir as mybir
from concourse.bass_utils import run_bass_kernel_spmd

F32 = mybir.dt.float32
BF16 = mybir.dt.bfloat16
U32 = mybir.dt.uint32
ALU = mybir.AluOpType
AF = mybir.ActivationFunctionType
AX = mybir.AxisListType

T = 2048
D = 1024
H = 16
DH = 64
FF = 4096
E = 8
ND = D // 128
NQ = 4
TQ = T // NQ
NT = T // 128
NFC = FF // 128
CEXP = 12.0
LN_EPS = 1e-5
LN_EPS_L1F = 1e-12

ALL_BLOCKS = ("l0a", "l0f", "l1a", "l1f")


class _Cnt:
    def __init__(self, sem):
        self.sem = sem
        self.n = 0

    def tick(self, inst, k=1):
        inst.then_inc(self.sem, k)
        self.n += k
        return self.n


class Env:
    def __init__(self, nc, ctx):
        self.nc = nc
        self.ctx = ctx
        self.uid = 0
        self.te = _Cnt(ctx.enter_context(nc.semaphore("s_te")))
        self.ve = _Cnt(ctx.enter_context(nc.semaphore("s_ve")))
        self.se = _Cnt(ctx.enter_context(nc.semaphore("s_se")))
        self.ge = _Cnt(ctx.enter_context(nc.semaphore("s_ge")))
        self.qs = _Cnt(ctx.enter_context(nc.semaphore("s_qs")))
        self.qa = _Cnt(ctx.enter_context(nc.semaphore("s_qa")))
        self.qg = _Cnt(ctx.enter_context(nc.semaphore("s_qg")))
        self.sb = {}

    def name(self, base):
        self.uid += 1
        return f"{base}_{self.uid}"

    def ld_s(self, out, in_):
        return self.qs.tick(self.nc.sync.dma_start(out, in_), 16)

    def ld_a(self, out, in_):
        return self.qa.tick(self.nc.scalar.dma_start(out, in_), 16)

    def ld_g(self, out, in_, **kw):
        return self.qg.tick(self.nc.gpsimd.dma_start(out, in_, **kw), 16)

    def fence_dma(self, eng):
        for c in (self.te, self.ve, self.se, self.ge, self.qs, self.qa, self.qg):
            eng.wait_ge(c.sem, c.n)

    def barrier(self):
        nc = self.nc
        for eng in (nc.tensor, nc.vector, nc.scalar, nc.gpsimd, nc.sync):
            for c in (self.te, self.ve, self.se, self.ge, self.qs, self.qa, self.qg):
                eng.wait_ge(c.sem, c.n)


def _host_prep(inputs):
    d = {}
    f32 = np.float32

    wkqv = np.asarray(inputs["L0_attn_w_kqv"], f32).reshape(H, DH, 3, D)
    d["w0q"] = np.ascontiguousarray(wkqv[:, :, 1, :].reshape(H * DH, D).T)
    d["w0k"] = np.ascontiguousarray(wkqv[:, :, 0, :].reshape(H * DH, D).T)
    d["w0v"] = np.ascontiguousarray(wkqv[:, :, 2, :].reshape(H * DH, D).T)
    d["w0p"] = np.ascontiguousarray(np.asarray(inputs["L0_attn_w_proj"], f32).T)
    d["b0p"] = np.ascontiguousarray(np.asarray(inputs["L0_attn_b_proj"], f32).reshape(ND, 128).T)

    d["w0r"] = np.ascontiguousarray(np.asarray(inputs["L0_ffn_Wr"], f32))
    d["w0f1"] = np.ascontiguousarray(np.asarray(inputs["L0_ffn_W1"], f32).reshape(E * D, FF))
    d["w0f2"] = np.ascontiguousarray(np.asarray(inputs["L0_ffn_W2"], f32).reshape(E * FF, D))
    b0f1 = np.zeros((128, FF), f32)
    b0f1[:E] = np.asarray(inputs["L0_ffn_b1"], f32)
    d["b0f1"] = b0f1
    b0f2 = np.zeros((128, D), f32)
    b0f2[:E] = np.asarray(inputs["L0_ffn_b2"], f32)
    d["b0f2"] = b0f2

    perm = np.concatenate([np.arange(0, DH, 2), np.arange(1, DH, 2)])
    wq = np.asarray(inputs["L1_attn_wq"], f32).reshape(D, H, DH)[:, :, perm].reshape(D, D)
    wk = np.asarray(inputs["L1_attn_wk"], f32).reshape(D, H, DH)[:, :, perm].reshape(D, D)
    d["w1q"] = np.ascontiguousarray(wq)
    d["w1k"] = np.ascontiguousarray(wk)
    d["w1v"] = np.ascontiguousarray(np.asarray(inputs["L1_attn_wv"], f32))
    d["w1p"] = np.ascontiguousarray(np.asarray(inputs["L1_attn_wo_w"], f32))
    d["b1p"] = np.ascontiguousarray(np.asarray(inputs["L1_attn_wo_b"], f32).reshape(ND, 128).T)

    d["w1r"] = np.ascontiguousarray(np.asarray(inputs["L1_ffn_Wr"], f32).T)
    d["w1f1"] = np.ascontiguousarray(np.asarray(inputs["L1_ffn_W1"], f32))
    b1f1 = np.zeros((128, FF), f32)
    b1f1[:E] = np.asarray(inputs["L1_ffn_b1"], f32)
    d["b1f1"] = b1f1
    d["w1fo"] = np.ascontiguousarray(np.asarray(inputs["L1_ffn_Wo"], f32))
    d["b1fo"] = np.ascontiguousarray(np.asarray(inputs["L1_ffn_bo"], f32).reshape(ND, 128).T)
    d["g1f"] = np.ascontiguousarray(np.asarray(inputs["L1_ffn_gamma"], f32).reshape(ND, 128).T)
    d["be1f"] = np.ascontiguousarray(np.asarray(inputs["L1_ffn_beta"], f32).reshape(ND, 128).T)

    d["c_ones"] = np.ones((1, TQ), f32)
    d["c_ocol"] = np.ones((128, 1), f32)
    d["c_ident"] = np.eye(128, dtype=f32)
    d["c_iota8"] = np.arange(8, dtype=f32)[None, :]
    masks = np.zeros((128, 4 * TQ), f32)
    negm = np.zeros((128, 4 * TQ), f32)
    i = np.arange(128)[:, None]
    j = np.arange(TQ)[None, :]
    for oo in range(4):
        allowed = (j <= i + 128 * oo)
        masks[:, oo * TQ:(oo + 1) * TQ] = allowed.astype(f32)
        negm[:, oo * TQ:(oo + 1) * TQ] = np.where(allowed, 0.0, -1e30).astype(f32)
    d["c_masks"] = masks
    d["c_negm"] = negm
    csc = np.zeros((128, 4), f32)
    csc[:, 0] = -CEXP
    csc[:, 1] = LN_EPS
    csc[:, 3] = LN_EPS_L1F
    d["c_csc"] = csc
    inv = 10000.0 ** (-np.arange(32, dtype=np.float64) * 2.0 / DH)
    ang = np.outer(np.arange(T, dtype=np.float64), inv)
    cosv = np.cos(ang).T.astype(f32)
    sinv = np.sin(ang).T.astype(f32)
    ropc = np.zeros((128, T), f32)
    rops = np.zeros((128, T), f32)
    for g in range(4):
        ropc[g * 32:(g + 1) * 32] = cosv
        rops[g * 32:(g + 1) * 32] = sinv * (-1.0 if (g % 2 == 0) else 1.0)
    d["c_ropc"] = ropc
    d["c_rops"] = rops
    e8 = np.zeros((128, 8 * 128), f32)
    for e in range(8):
        e8[e, e * 128:(e + 1) * 128] = 1.0
    d["c_e8"] = e8

    x = np.asarray(inputs["L0_attn_x"], f32)
    xs = [np.ascontiguousarray(x[b].T) for b in range(x.shape[0])]
    return d, xs


def _load_consts(env):
    nc, ctx, sb, dram = env.nc, env.ctx, env.sb, env.dram
    sb["ONES"] = ctx.enter_context(nc.sbuf_tensor("c_ones_sb", [1, TQ], BF16))
    sb["OCOL"] = ctx.enter_context(nc.sbuf_tensor("c_ocol_sb", [128, 1], BF16))
    sb["IDENT"] = ctx.enter_context(nc.sbuf_tensor("c_ident_sb", [128, 128], BF16))
    sb["IOTA8"] = ctx.enter_context(nc.sbuf_tensor("c_iota8_sb", [1, 8], F32))
    sb["NEGM"] = ctx.enter_context(nc.sbuf_tensor("c_negm_sb", [128, 4 * TQ], F32))
    sb["CSC"] = ctx.enter_context(nc.sbuf_tensor("c_csc_sb", [128, 4], F32))
    sb["E8"] = ctx.enter_context(nc.sbuf_tensor("c_e8_sb", [128, 8 * 128], BF16))
    sb["ONESF"] = ctx.enter_context(nc.sbuf_tensor("c_onesf_sb", [1, TQ], F32))
    sb["IDENTF"] = ctx.enter_context(nc.sbuf_tensor("c_identf_sb", [128, 128], F32))
    sb["OCOLF"] = ctx.enter_context(nc.sbuf_tensor("c_ocolf_sb", [128, 1], F32))

    env.ld_g(sb["ONES"][:, :], dram["c_ones"][:, :])
    env.ld_g(sb["OCOL"][:, :], dram["c_ocol"][:, :])
    env.ld_g(sb["IDENT"][:, :], dram["c_ident"][:, :])
    env.ld_s(sb["IOTA8"][:, :], dram["c_iota8"][:, :])
    env.ld_g(sb["NEGM"][:, :], dram["c_negm"][:, :])
    env.ld_s(sb["CSC"][:, :], dram["c_csc"][:, :])
    env.ld_g(sb["E8"][:, :], dram["c_e8"][:, :])
    env.ld_s(sb["ONESF"][:, :], dram["c_ones"][:, :])
    env.ld_s(sb["IDENTF"][:, :], dram["c_ident"][:, :])
    env.ld_s(sb["OCOLF"][:, :], dram["c_ocol"][:, :])
    env.consts_qg = env.qg.n
    env.consts_qs = env.qs.n


def _alloc_ln(env, scope, want_out=False):
    nc = env.nc
    n = env.name
    sc = {}
    sc["XB"] = scope.enter_context(nc.sbuf_tensor(n("ln_xb"), [128, ND * TQ], BF16))
    sc["XSQ"] = scope.enter_context(nc.sbuf_tensor(n("ln_xsq"), [128, ND * TQ], BF16))
    sc["RM"] = scope.enter_context(nc.sbuf_tensor(n("ln_rm"), [1, TQ], F32))
    sc["RT"] = scope.enter_context(nc.sbuf_tensor(n("ln_rt"), [1, TQ], F32))
    sc["RV"] = scope.enter_context(nc.sbuf_tensor(n("ln_rv"), [1, TQ], F32))
    sc["RL"] = scope.enter_context(nc.sbuf_tensor(n("ln_rl"), [1, TQ], F32))
    sc["RRb"] = scope.enter_context(nc.sbuf_tensor(n("ln_rrb"), [1, TQ], BF16))
    sc["RAb"] = scope.enter_context(nc.sbuf_tensor(n("ln_rab"), [1, TQ], BF16))
    sc["RRf"] = scope.enter_context(nc.sbuf_tensor(n("ln_rrf"), [1, TQ], F32))
    sc["RAf"] = scope.enter_context(nc.sbuf_tensor(n("ln_raf"), [1, TQ], F32))
    sc["RBS"] = scope.enter_context(nc.sbuf_tensor(n("ln_rbs"), [128, TQ], BF16))
    sc["ABS"] = scope.enter_context(nc.sbuf_tensor(n("ln_abs"), [128, TQ], BF16))
    if want_out:
        sc["XSQ2"] = scope.enter_context(nc.sbuf_tensor(n("ln_xsq2"), [128, ND * TQ], F32))
    sc["PS1"] = scope.enter_context(nc.psum_tensor(n("ln_ps1"), [1, TQ], F32))
    sc["PS2"] = scope.enter_context(nc.psum_tensor(n("ln_ps2"), [1, TQ], F32))
    sc["PRB"] = scope.enter_context(nc.psum_tensor(n("ln_prb"), [128, TQ], F32))
    sc["PAB"] = scope.enter_context(nc.psum_tensor(n("ln_pab"), [128, TQ], F32))
    return sc


def _emit_ln(env, ln_sc, rs_ap, dst_xln, eps_col=1, gamma=None, beta=None, out_f32=None):
    nc, sb = env.nc, env.sb
    XB, XSQ = ln_sc["XB"], ln_sc["XSQ"]
    RM, RT, RV, RL, RRb, RAb = (ln_sc[k] for k in ("RM", "RT", "RV", "RL", "RRb", "RAb"))
    del RRb, RAb
    RRb, RAb = ln_sc["RRb"], ln_sc["RAb"]
    PS1, PS2, PRB, PAB = (ln_sc[k] for k in ("PS1", "PS2", "PRB", "PAB"))
    RBS, ABS = ln_sc["RBS"], ln_sc["ABS"]

    nc.gpsimd.wait_ge(env.ve.sem, env.ve.n)
    for dc in range(ND):
        nc.gpsimd.tensor_copy(XB[:, dc * TQ:(dc + 1) * TQ], rs_ap(dc))
    for dc in range(ND):
        inst = nc.gpsimd.tensor_tensor(XSQ[:, dc * TQ:(dc + 1) * TQ], XB[:, dc * TQ:(dc + 1) * TQ],
                                       XB[:, dc * TQ:(dc + 1) * TQ], op=ALU.mult)
    env.ge.tick(inst)

    nc.tensor.wait_ge(env.ge.sem, env.ge.n)
    nc.tensor.wait_ge(env.ve.sem, env.ve.n)
    for dc in range(ND):
        nc.tensor.matmul(PS1[:, :], sb["OCOL"][:, :], XB[:, dc * TQ:(dc + 1) * TQ],
                         start=(dc == 0), stop=(dc == ND - 1))
    for dc in range(ND):
        inst = nc.tensor.matmul(PS2[:, :], sb["OCOL"][:, :], XSQ[:, dc * TQ:(dc + 1) * TQ],
                                start=(dc == 0), stop=(dc == ND - 1))
    env.te.tick(inst)

    nc.vector.wait_ge(env.te.sem, env.te.n)
    nc.vector.tensor_scalar(RM[:, :], PS1[:, :], 1.0 / D, None, op0=ALU.mult)
    nc.vector.tensor_scalar(RT[:, :], PS2[:, :], 1.0 / D, None, op0=ALU.mult)
    nc.vector.drain()
    nc.vector.tensor_tensor(RV[:, :], RM[:, :], RM[:, :], op=ALU.mult)
    nc.vector.drain()
    inst = nc.vector.tensor_tensor(RV[:, :], RT[:, :], RV[:, :], op=ALU.subtract)
    env.ve.tick(inst)

    RRf, RAf = ln_sc["RRf"], ln_sc["RAf"]
    nc.scalar.wait_ge(env.ve.sem, env.ve.n)
    nc.scalar.activation(RL[:, :], RV[:, :], AF.Ln, bias=sb["CSC"][0:1, eps_col:eps_col + 1])
    nc.scalar.drain()
    nc.scalar.activation(RRf[:, :], RL[:, :], AF.Exp, scale=-0.5)
    nc.scalar.drain()
    inst = nc.scalar.copy(RRb[:, :], RRf[:, :])
    env.se.tick(inst)

    nc.vector.wait_ge(env.se.sem, env.se.n)
    nc.vector.scalar_tensor_tensor(RAf[:, :], RM[:, :], -1.0, RRf[:, :], op0=ALU.mult, op1=ALU.mult)
    nc.vector.drain()
    inst = nc.vector.tensor_copy(RAb[:, :], RAf[:, :])
    env.ve.tick(inst)

    nc.tensor.wait_ge(env.ve.sem, env.ve.n)
    nc.tensor.wait_ge(env.se.sem, env.se.n)
    nc.tensor.matmul(PRB[:, :], sb["ONES"][0:1, 0:128], RRb[:, :], start=True, stop=True)
    inst = nc.tensor.matmul(PAB[:, :], sb["ONES"][0:1, 0:128], RAb[:, :], start=True, stop=True)
    env.te.tick(inst)

    nc.scalar.wait_ge(env.te.sem, env.te.n)
    nc.scalar.copy(RBS[:, :], PRB[:, :])
    inst = nc.scalar.copy(ABS[:, :], PAB[:, :])
    env.se.tick(inst)

    nc.vector.wait_ge(env.se.sem, env.se.n)
    for dc in range(ND):
        nc.vector.tensor_tensor(XB[:, dc * TQ:(dc + 1) * TQ], XB[:, dc * TQ:(dc + 1) * TQ],
                                RBS[:, :], op=ALU.mult)
    if gamma is None:
        for dc in range(ND):
            inst = nc.vector.tensor_tensor(dst_xln(dc), XB[:, dc * TQ:(dc + 1) * TQ], ABS[:, :], op=ALU.add)
    else:
        for dc in range(ND):
            nc.vector.tensor_tensor(XB[:, dc * TQ:(dc + 1) * TQ], XB[:, dc * TQ:(dc + 1) * TQ],
                                    ABS[:, :], op=ALU.add)
        nc.vector.drain()
        for dc in range(ND):
            inst = nc.vector.tensor_scalar(out_f32(dc), XB[:, dc * TQ:(dc + 1) * TQ],
                                           gamma(dc), beta(dc), op0=ALU.mult, op1=ALU.add)
    env.ve.tick(inst)


def _emit_attn(env, wq_key, wk_key, wv_key, wp_key, bp_key, rope, rs_src, rs_dst, dbg=None):
    nc, sb, dram = env.nc, env.sb, env.dram
    n = env.name
    NV = H * DH

    with ExitStack() as outer:
        KF = outer.enter_context(nc.sbuf_tensor(n("a_kf"), [128, ND * T], BF16))
        VT = outer.enter_context(nc.sbuf_tensor(n("a_vt"), [128, NT * NV], BF16))
        XLN = outer.enter_context(nc.sbuf_tensor(n("a_xln"), [128, ND * T], BF16))
        sb = dict(sb, XLN=XLN)

        with ExitStack() as sA:
            RS = sA.enter_context(nc.sbuf_tensor(n("a_rs"), [128, ND * T], F32))
            env.fence_dma(nc.sync)
            for dc in range(ND):
                env.ld_s(RS[:, dc * T:(dc + 1) * T], rs_src[dc * 128:(dc + 1) * 128, :])
            rs_loaded = env.qs.n
            ln_sc = _alloc_ln(env, sA)
            nc.gpsimd.wait_ge(env.qs.sem, rs_loaded)
            nc.gpsimd.wait_ge(env.qg.sem, env.consts_qg)
            nc.tensor.wait_ge(env.qs.sem, env.consts_qs)
            nc.tensor.wait_ge(env.qg.sem, env.consts_qg)
            for q in range(NQ):
                _emit_ln(env, ln_sc,
                         lambda dc, q=q: RS[:, dc * T + q * TQ: dc * T + (q + 1) * TQ],
                         lambda dc, q=q: sb["XLN"][:, dc * T + q * TQ: dc * T + (q + 1) * TQ])
            xln_ve = env.ve.n
            if rs_dst is not rs_src:
                nc.sync.wait_ge(env.qs.sem, rs_loaded)
                for dc in range(ND):
                    env.ld_s(rs_dst[dc * 128:(dc + 1) * 128, :], RS[:, dc * T:(dc + 1) * T])
            rs_stored = env.qs.n
            nc.sync.wait_ge(env.ge.sem, env.ge.n)

        with ExitStack() as sBC:
            if rope:
                ROPC = sBC.enter_context(nc.sbuf_tensor(n("a_ropc"), [128, T], BF16))
                ROPS = sBC.enter_context(nc.sbuf_tensor(n("a_rops"), [128, T], BF16))
                RSWP = sBC.enter_context(nc.sbuf_tensor(n("a_rswp"), [128, T], BF16))
                RTMP = sBC.enter_context(nc.sbuf_tensor(n("a_rtmp"), [128, T], BF16))
                env.fence_dma(nc.gpsimd)
                env.ld_g(ROPC[:, :], dram["c_ropc"][:, :])
                env.ld_g(ROPS[:, :], dram["c_rops"][:, :])
                rope_loaded = env.qg.n

            with ExitStack() as sB:
                WK = sB.enter_context(nc.sbuf_tensor(n("a_wk"), [128, ND * D], BF16))
                WV = sB.enter_context(nc.sbuf_tensor(n("a_wv"), [128, ND * NV], BF16))
                PK0 = sB.enter_context(nc.psum_tensor(n("a_pk0"), [128, TQ], F32))
                PK1 = sB.enter_context(nc.psum_tensor(n("a_pk1"), [128, TQ], F32))
                PV0 = sB.enter_context(nc.psum_tensor(n("a_pv0"), [128, NV], F32))
                PV1 = sB.enter_context(nc.psum_tensor(n("a_pv1"), [128, NV], F32))
                env.fence_dma(nc.gpsimd)
                for dc in range(ND):
                    env.ld_g(WK[:, dc * D:(dc + 1) * D], dram[wk_key][dc * 128:(dc + 1) * 128, :])
                for dc in range(ND):
                    env.ld_g(WV[:, dc * NV:(dc + 1) * NV], dram[wv_key][dc * 128:(dc + 1) * 128, :])
                wkv_loaded = env.qg.n

                nc.tensor.wait_ge(env.qg.sem, wkv_loaded)
                nc.tensor.wait_ge(env.ve.sem, xln_ve)
                se_base = env.se.n
                fill = 0
                for jc in range(ND):
                    for q in range(NQ):
                        pk = PK0 if (fill % 2 == 0) else PK1
                        if fill >= 2:
                            nc.tensor.wait_ge(env.se.sem, se_base + fill - 1)
                        for dc in range(ND):
                            inst = nc.tensor.matmul(pk[:, :],
                                                    WK[:, dc * D + jc * 128: dc * D + (jc + 1) * 128],
                                                    sb["XLN"][:, dc * T + q * TQ: dc * T + (q + 1) * TQ],
                                                    start=(dc == 0), stop=(dc == ND - 1))
                        env.te.tick(inst)
                        nc.scalar.wait_ge(env.te.sem, env.te.n)
                        inst = nc.scalar.copy(KF[:, jc * T + q * TQ: jc * T + (q + 1) * TQ], pk[:, :])
                        env.se.tick(inst)
                        fill += 1
                k_se = env.se.n

                if rope:
                    nc.vector.wait_ge(env.se.sem, k_se)
                    nc.vector.wait_ge(env.qg.sem, rope_loaded)
                    for jc in range(ND):
                        kf = KF[:, jc * T:(jc + 1) * T]
                        for g in range(4):
                            sp = (g // 2) * 64 + (1 - (g % 2)) * 32
                            nc.vector.tensor_copy(RSWP[g * 32:(g + 1) * 32, 0:T],
                                                  KF[sp:sp + 32, jc * T:(jc + 1) * T])
                        nc.vector.drain()
                        nc.vector.tensor_tensor(RTMP[:, 0:T], kf, ROPC[:, :], op=ALU.mult)
                        nc.vector.tensor_tensor(RSWP[:, 0:T], RSWP[:, 0:T], ROPS[:, :], op=ALU.mult)
                        nc.vector.drain()
                        inst = nc.vector.tensor_tensor(kf, RTMP[:, 0:T], RSWP[:, 0:T], op=ALU.add)
                        env.ve.tick(inst)
                    nc.tensor.wait_ge(env.ve.sem, env.ve.n)

                se_base_v = env.se.n
                for tc in range(NT):
                    pv = PV0 if (tc % 2 == 0) else PV1
                    if tc >= 2:
                        nc.tensor.wait_ge(env.se.sem, se_base_v + tc - 1)
                    for (c0, c1) in ((0, 512), (512, 1024)):
                        for dc in range(ND):
                            inst = nc.tensor.matmul(pv[:, c0:c1],
                                                    sb["XLN"][:, dc * T + tc * 128: dc * T + (tc + 1) * 128],
                                                    WV[:, dc * NV + c0: dc * NV + c1],
                                                    start=(dc == 0), stop=(dc == ND - 1))
                    env.te.tick(inst)
                    nc.scalar.wait_ge(env.te.sem, env.te.n)
                    inst = nc.scalar.copy(VT[:, tc * NV:(tc + 1) * NV], pv[:, :])
                    env.se.tick(inst)

            with ExitStack() as sC:
                WQ = sC.enter_context(nc.sbuf_tensor(n("a_wq"), [128, ND * D], BF16))
                WP = sC.enter_context(nc.sbuf_tensor(n("a_wp"), [128, ND * D], BF16))
                BP = sC.enter_context(nc.sbuf_tensor(n("a_bp"), [128, ND], F32))
                QF = sC.enter_context(nc.sbuf_tensor(n("a_qf"), [128, ND * TQ], BF16))
                ATT = sC.enter_context(nc.sbuf_tensor(n("a_att"), [128, NT * TQ], BF16))
                AQ = sC.enter_context(nc.sbuf_tensor(n("a_aq"), [128, T], BF16))
                CTXF = sC.enter_context(nc.sbuf_tensor(n("a_ctxf"), [128, ND * TQ], BF16))
                PRJ = sC.enter_context(nc.sbuf_tensor(n("a_prj"), [128, 2 * TQ], F32))
                MC = sC.enter_context(nc.sbuf_tensor(n("a_mc"), [128, 4], F32))
                PSC = sC.enter_context(nc.psum_tensor(n("a_psc"), [128, T], F32))
                PT0 = sC.enter_context(nc.psum_tensor(n("a_pt0"), [128, 1024], BF16))
                PT1 = sC.enter_context(nc.psum_tensor(n("a_pt1"), [128, 1024], BF16))
                PC0 = sC.enter_context(nc.psum_tensor(n("a_pc0"), [64, TQ], F32))
                PC1 = sC.enter_context(nc.psum_tensor(n("a_pc1"), [64, TQ], F32))

                env.fence_dma(nc.gpsimd)
                for dc in range(ND):
                    env.ld_g(WQ[:, dc * D:(dc + 1) * D], dram[wq_key][dc * 128:(dc + 1) * 128, :])
                for dc in range(ND):
                    env.ld_g(WP[:, dc * D:(dc + 1) * D], dram[wp_key][dc * 128:(dc + 1) * 128, :])
                wqp_loaded = env.qg.n
                env.fence_dma(nc.sync)
                env.ld_s(BP[:, :], dram[bp_key][:, :])
                bp_loaded = env.qs.n

                nc.tensor.wait_ge(env.qg.sem, wqp_loaded)
                qc_range = range(NQ) if dbg is None else range(int(dbg.split(":")[1]), int(dbg.split(":")[1]) + 1)
                for qc in qc_range:
                    kmax = 4 * qc + 4
                    kn = qc + 1
                    kw = kn * TQ
                    nc.tensor.wait_ge(env.se.sem, env.se.n)
                    nc.tensor.wait_ge(env.ve.sem, env.ve.n)
                    se_base_q = env.se.n
                    for jcq in range(ND):
                        pq = PSC[:, 0:TQ] if (jcq % 2 == 0) else PSC[:, TQ:2 * TQ]
                        if jcq >= 2:
                            nc.tensor.wait_ge(env.se.sem, se_base_q + jcq - 1)
                        for dc in range(ND):
                            inst = nc.tensor.matmul(pq,
                                                    WQ[:, dc * D + jcq * 128: dc * D + (jcq + 1) * 128],
                                                    sb["XLN"][:, dc * T + qc * TQ: dc * T + (qc + 1) * TQ],
                                                    start=(dc == 0), stop=(dc == ND - 1))
                        env.te.tick(inst)
                        nc.scalar.wait_ge(env.te.sem, env.te.n)
                        inst = nc.scalar.copy(QF[:, jcq * TQ:(jcq + 1) * TQ], pq)
                        env.se.tick(inst)
                    q_se = env.se.n

                    if rope:
                        nc.vector.wait_ge(env.se.sem, q_se)
                        for jcq in range(ND):
                            qf = QF[:, jcq * TQ:(jcq + 1) * TQ]
                            for g in range(4):
                                sp = (g // 2) * 64 + (1 - (g % 2)) * 32
                                nc.vector.tensor_copy(RSWP[g * 32:(g + 1) * 32, 0:TQ],
                                                      QF[sp:sp + 32, jcq * TQ:(jcq + 1) * TQ])
                            nc.vector.drain()
                            nc.vector.tensor_tensor(RTMP[:, 0:TQ], qf,
                                                    ROPC[:, qc * TQ:(qc + 1) * TQ], op=ALU.mult)
                            nc.vector.tensor_tensor(RSWP[:, 0:TQ], RSWP[:, 0:TQ],
                                                    ROPS[:, qc * TQ:(qc + 1) * TQ], op=ALU.mult)
                            nc.vector.drain()
                            inst = nc.vector.tensor_tensor(qf, RTMP[:, 0:TQ], RSWP[:, 0:TQ], op=ALU.add)
                            env.ve.tick(inst)

                    for h in range(H):
                        jc = h // 2
                        pp = (h % 2) * 64
                        for ql in range(4):
                            nc.tensor.wait_ge(env.se.sem, env.se.n)
                            nc.tensor.wait_ge(env.ve.sem, env.ve.n)
                            for kk in range(kn):
                                inst = nc.tensor.matmul(
                                    PSC[:, kk * TQ:(kk + 1) * TQ],
                                    QF[pp:pp + 64, jc * TQ + ql * 128: jc * TQ + (ql + 1) * 128],
                                    KF[pp:pp + 64, jc * T + kk * TQ: jc * T + (kk + 1) * TQ],
                                    start=True, stop=True)
                            env.te.tick(inst)
                            nc.vector.wait_ge(env.te.sem, env.te.n)
                            nc.vector.tensor_tensor(
                                PSC[:, qc * TQ:(qc + 1) * TQ], PSC[:, qc * TQ:(qc + 1) * TQ],
                                sb["NEGM"][:, ql * TQ:(ql + 1) * TQ], op=ALU.add)
                            nc.vector.drain()
                            nc.vector.reduce_max(MC[:, 0:1], PSC[:, 0:kw], axis=AX.X)
                            nc.vector.drain()
                            inst = nc.vector.tensor_scalar(MC[:, 1:2], MC[:, 0:1], -0.125, None, op0=ALU.mult)
                            env.ve.tick(inst)
                            nc.scalar.wait_ge(env.ve.sem, env.ve.n)
                            inst = nc.scalar.activation(AQ[:, 0:kw], PSC[:, 0:kw], AF.Exp,
                                                        scale=0.125, bias=MC[:, 1:2],
                                                        accum_out=MC[:, 2:3])
                            env.se.tick(inst)
                            nc.vector.wait_ge(env.se.sem, env.se.n)
                            nc.vector.reciprocal(MC[:, 3:4], MC[:, 2:3])
                            nc.vector.drain()
                            inst = nc.vector.tensor_scalar(AQ[:, 0:kw], AQ[:, 0:kw], MC[:, 3:4],
                                                           None, op0=ALU.mult)
                            env.ve.tick(inst)
                            norm_ve = env.ve.n
                            nc.tensor.wait_ge(env.ve.sem, norm_ve)
                            for grp in range((kmax + 7) // 8):
                                ptb = PT0 if (grp % 2 == 0) else PT1
                                nt_ = min(kmax, grp * 8 + 8) - grp * 8
                                nc.tensor.wait_ge(env.ve.sem, env.ve.n)
                                for ti in range(nt_):
                                    kc = grp * 8 + ti
                                    inst = nc.tensor.transpose(ptb[:, ti * 128:(ti + 1) * 128],
                                                               AQ[:, kc * 128:(kc + 1) * 128],
                                                               sb["IDENT"][:, :])
                                env.te.tick(inst)
                                nc.vector.wait_ge(env.te.sem, env.te.n)
                                dst = ATT[:, :].rearrange("p (k q) -> p k q", q=TQ)[
                                    :, grp * 8: grp * 8 + nt_, ql * 128:(ql + 1) * 128]
                                inst = nc.vector.tensor_copy(dst, ptb[:, 0:nt_ * 128])
                                env.ve.tick(inst)
                        nc.tensor.wait_ge(env.ve.sem, env.ve.n)
                        nc.tensor.wait_ge(env.se.sem, env.se.n)
                        pc = PC0 if (h % 2 == 0) else PC1
                        for kc in range(kmax):
                            inst = nc.tensor.matmul(pc[0:64, :],
                                                    VT[:, kc * NV + h * DH: kc * NV + (h + 1) * DH],
                                                    ATT[:, kc * TQ:(kc + 1) * TQ],
                                                    start=(kc == 0), stop=(kc == kmax - 1))
                        env.te.tick(inst)
                        nc.scalar.wait_ge(env.te.sem, env.te.n)
                        inst = nc.scalar.copy(CTXF[pp:pp + 64, jc * TQ:(jc + 1) * TQ], pc[0:64, :])
                        env.se.tick(inst)
                        if dbg is not None and h == int(dbg.split(":")[2]):
                            nc.vector.wait_ge(env.se.sem, env.se.n)
                            DBG = sC.enter_context(nc.sbuf_tensor(n("a_dbg"), [128, kmax * TQ], F32))
                            for kc in range(kmax):
                                inst = nc.vector.tensor_copy(DBG[:, kc * TQ:(kc + 1) * TQ], ATT[:, kc * TQ:(kc + 1) * TQ])
                            env.ve.tick(inst)
                            nc.gpsimd.wait_ge(env.ve.sem, env.ve.n)
                            nc.gpsimd.wait_ge(env.se.sem, env.se.n)
                            for kc in range(kmax):
                                env.ld_g(env.dram["outT"][0:128, kc * TQ:(kc + 1) * TQ], DBG[:, kc * TQ:(kc + 1) * TQ])
                            env.ld_g(env.dram["outT"][128:192, 0:TQ], CTXF[pp:pp + 64, jc * TQ:(jc + 1) * TQ])
                            nc.sync.wait_ge(env.qg.sem, env.qg.n)
                            env.barrier()
                            return
                    ctx_se = env.se.n

                    nc.tensor.wait_ge(env.se.sem, ctx_se)
                    nc.scalar.wait_ge(env.qs.sem, bp_loaded)
                    nc.gpsimd.wait_ge(env.qs.sem, rs_stored)
                    se_base_p = env.se.n
                    qg_base_p = env.qg.n
                    for jcp in range(ND):
                        pq = PSC[:, 0:TQ] if (jcp % 2 == 0) else PSC[:, TQ:2 * TQ]
                        if jcp >= 2:
                            nc.tensor.wait_ge(env.se.sem, se_base_p + jcp - 1)
                        for cc in range(ND):
                            inst = nc.tensor.matmul(pq,
                                                    WP[:, cc * D + jcp * 128: cc * D + (jcp + 1) * 128],
                                                    CTXF[:, cc * TQ:(cc + 1) * TQ],
                                                    start=(cc == 0), stop=(cc == ND - 1))
                        env.te.tick(inst)
                        nc.scalar.wait_ge(env.te.sem, env.te.n)
                        if jcp >= 2:
                            nc.scalar.wait_ge(env.qg.sem, qg_base_p + 16 * (jcp - 1))
                        inst = nc.scalar.activation(PRJ[:, (jcp % 2) * TQ:(jcp % 2 + 1) * TQ], pq,
                                                    AF.Identity, bias=BP[:, jcp:jcp + 1])
                        env.se.tick(inst)
                        nc.gpsimd.wait_ge(env.se.sem, env.se.n)
                        env.ld_g(rs_dst[jcp * 128:(jcp + 1) * 128, qc * TQ:(qc + 1) * TQ],
                                 PRJ[:, (jcp % 2) * TQ:(jcp % 2 + 1) * TQ], accum_op=ALU.add)
                    nc.scalar.wait_ge(env.qg.sem, env.qg.n)

                env.barrier()



def _emit_ffn(env, l1f, rs_src, rs_dst, dbg=None):
    nc, sb, dram = env.nc, env.sb, env.dram
    n = env.name
    NSL = 8 if l1f else 2

    with ExitStack() as outer:
        if l1f:
            MASKTP = outer.enter_context(nc.sbuf_tensor(n("f_masktp"), [128, T], BF16))
            WRT = outer.enter_context(nc.sbuf_tensor(n("f_wrt"), [128, ND * 8], F32))
            CSUM = outer.enter_context(nc.sbuf_tensor(n("f_csum"), [1, 8], F32))
            RSTC = outer.enter_context(nc.sbuf_tensor(n("f_rstc"), [128, 1], F32))
            LGT = outer.enter_context(nc.sbuf_tensor(n("f_lgt"), [128, 8], F32))
            RMX = outer.enter_context(nc.sbuf_tensor(n("f_rmx"), [128, 1], F32))
            MTOK = outer.enter_context(nc.sbuf_tensor(n("f_mtok"), [128, 8], BF16))
            B1SB = outer.enter_context(nc.sbuf_tensor(n("f_b1sb"), [128, FF], BF16))
            BOC = outer.enter_context(nc.sbuf_tensor(n("f_boc"), [128, ND], F32))
            GAC = outer.enter_context(nc.sbuf_tensor(n("f_gac"), [128, ND], F32))
            BEC = outer.enter_context(nc.sbuf_tensor(n("f_bec"), [128, ND], F32))
        else:
            PLS = outer.enter_context(nc.sbuf_tensor(n("f_pls"), [128, ND * NQ], F32))
            DUMP = outer.enter_context(nc.sbuf_tensor(n("f_dump"), [128, TQ], BF16))
            WRR = outer.enter_context(nc.sbuf_tensor(n("f_wrr"), [128, ND * 8], F32))
            B1SB = outer.enter_context(nc.sbuf_tensor(n("f_b1sb"), [128, FF], BF16))
            B2SB = outer.enter_context(nc.sbuf_tensor(n("f_b2sb"), [128, D], BF16))
            B1C = outer.enter_context(nc.sbuf_tensor(n("f_b1c"), [128, 64], F32))
            B2C = outer.enter_context(nc.sbuf_tensor(n("f_b2c"), [128, ND], F32))
            WEC = outer.enter_context(nc.sbuf_tensor(n("f_wec"), [128, 2], F32))
            EBC = outer.enter_context(nc.sbuf_tensor(n("f_ebc"), [128, 2], F32))

        env.fence_dma(nc.sync)
        if l1f:
            for dc in range(ND):
                env.ld_s(WRT[:, dc * 8:(dc + 1) * 8], dram["w1r"][dc * 128:(dc + 1) * 128, :])
            env.fence_dma(nc.gpsimd)
            env.ld_g(B1SB[:, :], dram["b1f1"][:, :])
            env.ld_s(BOC[:, :], dram["b1fo"][:, :])
            env.ld_s(GAC[:, :], dram["g1f"][:, :])
            env.ld_s(BEC[:, :], dram["be1f"][:, :])
        else:
            for dc in range(ND):
                env.ld_s(WRR[:, dc * 8:(dc + 1) * 8], dram["w0r"][dc * 128:(dc + 1) * 128, :])
            env.fence_dma(nc.gpsimd)
            env.ld_g(B1SB[:, :], dram["b0f1"][:, :])
            env.ld_g(B2SB[:, :], dram["b0f2"][:, :])
        pre_qs = env.qs.n
        pre_qg = env.qg.n

        if l1f:
            with ExitStack() as sP:
                CT0 = sP.enter_context(nc.sbuf_tensor(n("f_ct0"), [128, FF], BF16))
                CT1 = sP.enter_context(nc.sbuf_tensor(n("f_ct1"), [128, FF], BF16))
                env.fence_dma(nc.gpsimd)
                w1flat = dram["w1f1"].rearrange("e d f -> (e d) f")
                cts = [CT0, CT1]
                store_n = {}
                for i in range(E * ND + NFC):
                    ct = cts[i % 2]
                    if i >= 2:
                        nc.gpsimd.wait_ge(env.qa.sem, store_n[i - 2])
                    if i < E * ND:
                        env.ld_g(ct[:, :], w1flat[i * 128:(i + 1) * 128, :])
                        nc.scalar.wait_ge(env.qg.sem, env.qg.n)
                        inst = nc.scalar.dma_start(dram["s_w1x"][i * 128:(i + 1) * 128, :], ct[:, :])
                    else:
                        j = i - E * ND
                        env.ld_g(ct[:, 0:1024], dram["w1fo"][j * 128:(j + 1) * 128, :])
                        nc.scalar.wait_ge(env.qg.sem, env.qg.n)
                        inst = nc.scalar.dma_start(dram["s_wo"][j * 128:(j + 1) * 128, :], ct[:, 0:1024])
                    env.qa.tick(inst, 16)
                    store_n[i] = env.qa.n
                nc.gpsimd.wait_ge(env.qa.sem, env.qa.n)

        with ExitStack() as sA:
            RS = sA.enter_context(nc.sbuf_tensor(n("f_rs"), [128, ND * T], F32))
            XBUF = sA.enter_context(nc.sbuf_tensor(n("f_xbuf"), [128, ND * TQ], BF16))
            env.fence_dma(nc.sync)
            for dc in range(ND):
                env.ld_s(RS[:, dc * T:(dc + 1) * T], rs_src[dc * 128:(dc + 1) * 128, :])
            rs_loaded = env.qs.n
            ln_sc = _alloc_ln(env, sA)
            if l1f:
                PR = sA.enter_context(nc.psum_tensor(n("f_pr"), [128, 8], F32))
                PTR = sA.enter_context(nc.psum_tensor(n("f_ptr"), [128, 1], F32))
                PMT = sA.enter_context(nc.psum_tensor(n("f_pmt"), [8, 128], BF16))
                nc.vector.wait_ge(env.qg.sem, env.consts_qg)
                inst = nc.vector.memset(MASKTP[:, :], 0.0)
                env.ve.tick(inst)
                nc.tensor.wait_ge(env.qs.sem, pre_qs)
                nc.tensor.wait_ge(env.qg.sem, env.consts_qg)
                for dc in range(ND):
                    inst = nc.tensor.matmul(PMT[0:1, 0:8] if False else PR[0:1, 0:8],
                                            sb["OCOLF"][:, :], WRT[:, dc * 8:(dc + 1) * 8],
                                            start=(dc == 0), stop=(dc == ND - 1))
                env.te.tick(inst)
                nc.scalar.wait_ge(env.te.sem, env.te.n)
                inst = nc.scalar.copy(CSUM[:, :], PR[0:1, 0:8])
                env.se.tick(inst)
            nc.gpsimd.wait_ge(env.qs.sem, rs_loaded)
            nc.gpsimd.wait_ge(env.qg.sem, env.consts_qg)
            nc.tensor.wait_ge(env.qs.sem, env.consts_qs)
            nc.tensor.wait_ge(env.qg.sem, env.consts_qg)
            for q in range(NQ):
                _emit_ln(env, ln_sc,
                         lambda dc, q=q: RS[:, dc * T + q * TQ: dc * T + (q + 1) * TQ],
                         lambda dc: XBUF[:, dc * TQ:(dc + 1) * TQ])
                xq_ve = env.ve.n
                nc.scalar.wait_ge(env.ve.sem, xq_ve)
                for dc in range(ND):
                    inst = nc.scalar.dma_start(dram["s_xln"][dc * 128:(dc + 1) * 128, q * TQ:(q + 1) * TQ],
                                               XBUF[:, dc * TQ:(dc + 1) * TQ])
                    env.qa.tick(inst, 16)
                xq_stored = env.qa.n
                nc.vector.wait_ge(env.qa.sem, xq_stored)
                if not l1f:
                    nc.scalar.wait_ge(env.ve.sem, xq_ve)
                    for dc in range(ND):
                        inst = nc.scalar.activation(DUMP[:, :], XBUF[:, dc * TQ:(dc + 1) * TQ],
                                                    AF.Identity,
                                                    accum_out=PLS[:, dc * NQ + q: dc * NQ + q + 1])
                        env.se.tick(inst)
                    nc.vector.wait_ge(env.se.sem, env.se.n)
                else:
                    RRf, RAf = ln_sc["RRf"], ln_sc["RAf"]
                    for tc in range(4):
                        tg = q * 4 + tc
                        nc.tensor.wait_ge(env.se.sem, env.se.n)
                        nc.tensor.wait_ge(env.ve.sem, env.ve.n)
                        for dc in range(ND):
                            nc.tensor.matmul(PR[:, :],
                                             RS[:, dc * T + tg * 128: dc * T + (tg + 1) * 128],
                                             WRT[:, dc * 8:(dc + 1) * 8],
                                             start=(dc == 0), stop=False)
                        nc.tensor.matmul(PR[:, :], RAf[0:1, tc * 128:(tc + 1) * 128], CSUM[:, :],
                                         start=False, stop=True)
                        inst = nc.tensor.transpose(PTR[:, :], RRf[0:1, tc * 128:(tc + 1) * 128],
                                                   sb["IDENTF"][0:1, 0:1])
                        env.te.tick(inst)
                        nc.scalar.wait_ge(env.te.sem, env.te.n)
                        inst = nc.scalar.copy(RSTC[:, :], PTR[:, :])
                        env.se.tick(inst)
                        nc.vector.wait_ge(env.se.sem, env.se.n)
                        nc.vector.tensor_scalar(LGT[:, :], PR[:, :], RSTC[:, :], None, op0=ALU.mult)
                        nc.vector.drain()
                        nc.vector.reduce_max(RMX[:, :], LGT[:, :], axis=AX.X)
                        nc.vector.drain()
                        inst = nc.vector.tensor_scalar(MTOK[:, :], LGT[:, :], RMX[:, :], None, op0=ALU.is_ge)
                        env.ve.tick(inst)
                        nc.tensor.wait_ge(env.ve.sem, env.ve.n)
                        inst = nc.tensor.transpose(PMT[:, :], MTOK[:, :], sb["IDENT"][:, :])
                        env.te.tick(inst)
                        nc.scalar.wait_ge(env.te.sem, env.te.n)
                        nc.scalar.wait_ge(env.ve.sem, env.ve.n)
                        inst = nc.scalar.copy(MASKTP[0:8, tg * 128:(tg + 1) * 128], PMT[0:8, :])
                        env.se.tick(inst)
            if rs_dst is not rs_src:
                nc.sync.wait_ge(env.qs.sem, rs_loaded)
                for dc in range(ND):
                    env.ld_s(rs_dst[dc * 128:(dc + 1) * 128, :], RS[:, dc * T:(dc + 1) * T])
            rs_stored = env.qs.n
            nc.sync.wait_ge(env.ge.sem, env.ge.n)
            nc.sync.wait_ge(env.te.sem, env.te.n)

        if not l1f:
            with ExitStack() as sR:
                ROW8 = sR.enter_context(nc.sbuf_tensor(n("f_row8"), [1, 64], F32))
                ROWB = sR.enter_context(nc.sbuf_tensor(n("f_rowb"), [1, 24], BF16))
                PCOL = sR.enter_context(nc.sbuf_tensor(n("f_pcol"), [128, ND], F32))
                OHP = sR.enter_context(nc.sbuf_tensor(n("f_ohp"), [128, 3], BF16))
                IOTC = sR.enter_context(nc.sbuf_tensor(n("f_iotc"), [128, 2], F32))
                IOTU = sR.enter_context(nc.sbuf_tensor(n("f_iotu"), [128, 1], U32))
                IDXF = sR.enter_context(nc.sbuf_tensor(n("f_idxf"), [128, 96], F32))
                IDX = sR.enter_context(nc.sbuf_tensor(n("f_idx"), [128, 96], U32))
                GT0 = sR.enter_context(nc.sbuf_tensor(n("f_gt0"), [128, 2048], BF16))
                GT1 = sR.enter_context(nc.sbuf_tensor(n("f_gt1"), [128, 2048], BF16))
                PSL = sR.enter_context(nc.psum_tensor(n("f_psl"), [1, 64], F32))
                PB1 = sR.enter_context(nc.psum_tensor(n("f_pb1"), [128, 2], F32))
                POH = sR.enter_context(nc.psum_tensor(n("f_poh"), [8, 8], BF16))
                PWB = sR.enter_context(nc.psum_tensor(n("f_pwb"), [128, 2], F32))

                nc.vector.wait_ge(env.se.sem, env.se.n)
                for dc in range(ND):
                    nc.vector.tensor_tensor(PCOL[:, dc:dc + 1], PLS[:, dc * NQ:dc * NQ + 1],
                                            PLS[:, dc * NQ + 1:dc * NQ + 2], op=ALU.add)
                nc.vector.drain()
                for dc in range(ND):
                    nc.vector.tensor_tensor(IOTC[:, 0:1] if False else PCOL[:, dc:dc + 1],
                                            PCOL[:, dc:dc + 1], PLS[:, dc * NQ + 2:dc * NQ + 3], op=ALU.add)
                nc.vector.drain()
                for dc in range(ND):
                    nc.vector.tensor_tensor(PCOL[:, dc:dc + 1], PCOL[:, dc:dc + 1],
                                            PLS[:, dc * NQ + 3:dc * NQ + 4], op=ALU.add)
                nc.vector.drain()
                for dc in range(ND):
                    nc.vector.tensor_scalar(PCOL[:, dc:dc + 1], PCOL[:, dc:dc + 1], 1.0 / T, None, op0=ALU.mult)
                inst = nc.vector.nop()
                nc.vector.drain()
                env.ve.tick(nc.vector.tensor_copy(IOTC[:, 0:1], PCOL[:, 0:1]))

                nc.tensor.wait_ge(env.ve.sem, env.ve.n)
                nc.tensor.wait_ge(env.qs.sem, pre_qs)
                for dc in range(ND):
                    inst = nc.tensor.matmul(PSL[0:1, 0:8], PCOL[:, dc:dc + 1],
                                            WRR[:, dc * 8:(dc + 1) * 8],
                                            start=(dc == 0), stop=(dc == ND - 1))
                env.te.tick(inst)

                R = ROW8
                nc.vector.wait_ge(env.te.sem, env.te.n)
                nc.vector.tensor_copy(R[:, 0:8], PSL[0:1, 0:8])
                nc.vector.drain()
                nc.vector.reduce_max(R[:, 48:49], R[:, 0:8], axis=AX.X)
                nc.vector.drain()
                nc.vector.tensor_scalar(R[:, 8:16], R[:, 0:8], R[:, 48:49], None, op0=ALU.is_ge)
                nc.vector.drain()
                nc.vector.wait_ge(env.qs.sem, env.consts_qs)
                nc.vector.tensor_tensor(R[:, 32:40], R[:, 8:16], sb["IOTA8"][:, :], op=ALU.mult)
                nc.vector.drain()
                nc.vector.tensor_reduce(R[:, 50:51], R[:, 32:40], axis=AX.X, op=ALU.add)
                nc.vector.drain()
                nc.vector.scalar_tensor_tensor(R[:, 16:24], R[:, 8:16], -1e30, R[:, 0:8],
                                               op0=ALU.mult, op1=ALU.add)
                nc.vector.drain()
                nc.vector.reduce_max(R[:, 49:50], R[:, 16:24], axis=AX.X)
                nc.vector.drain()
                nc.vector.tensor_scalar(R[:, 24:32], R[:, 16:24], R[:, 49:50], None, op0=ALU.is_ge)
                nc.vector.drain()
                nc.vector.tensor_tensor(R[:, 32:40], R[:, 24:32], sb["IOTA8"][:, :], op=ALU.mult)
                nc.vector.drain()
                nc.vector.tensor_reduce(R[:, 51:52], R[:, 32:40], axis=AX.X, op=ALU.add)
                nc.vector.drain()
                inst = nc.vector.tensor_tensor(R[:, 52:53], R[:, 49:50], R[:, 48:49], op=ALU.subtract)
                env.ve.tick(inst)
                nc.scalar.wait_ge(env.ve.sem, env.ve.n)
                inst = nc.scalar.activation(R[:, 53:54], R[:, 52:53], AF.Exp)
                env.se.tick(inst)
                nc.vector.wait_ge(env.se.sem, env.se.n)
                nc.vector.tensor_scalar(R[:, 54:55], R[:, 53:54], 1.0, None, op0=ALU.add)
                nc.vector.drain()
                nc.vector.reciprocal(R[:, 55:56], R[:, 54:55])
                nc.vector.drain()
                nc.vector.tensor_scalar(R[:, 56:57], R[:, 55:56], -1.0, 1.0, op0=ALU.mult, op1=ALU.add)
                nc.vector.drain()
                nc.vector.tensor_scalar(R[:, 32:40], R[:, 8:16], R[:, 55:56], None, op0=ALU.mult)
                nc.vector.drain()
                nc.vector.tensor_scalar(R[:, 40:48], R[:, 24:32], R[:, 56:57], None, op0=ALU.mult)
                nc.vector.drain()
                nc.vector.tensor_tensor(R[:, 40:48], R[:, 40:48], R[:, 32:40], op=ALU.add)
                nc.vector.drain()
                nc.vector.tensor_copy(ROWB[:, 0:8], R[:, 8:16])
                nc.vector.tensor_copy(ROWB[:, 8:16], R[:, 24:32])
                inst = nc.vector.tensor_copy(ROWB[:, 16:24], R[:, 40:48])
                env.ve.tick(inst)

                nc.vector.drain()
                inst = nc.vector.memset(OHP[:, :], 0.0)
                env.ve.tick(inst)
                nc.tensor.wait_ge(env.ve.sem, env.ve.n)
                nc.tensor.transpose(POH[:, 0:1], ROWB[0:1, 0:8], sb["IDENT"][0:1, 0:1])
                nc.tensor.transpose(POH[:, 2:3], ROWB[0:1, 8:16], sb["IDENT"][0:1, 0:1])
                inst = nc.tensor.transpose(POH[:, 4:5], ROWB[0:1, 16:24], sb["IDENT"][0:1, 0:1])
                env.te.tick(inst)
                nc.scalar.wait_ge(env.te.sem, env.te.n)
                nc.scalar.copy(OHP[0:8, 0:1], POH[:, 0:1])
                nc.scalar.copy(OHP[0:8, 1:2], POH[:, 2:3])
                inst = nc.scalar.copy(OHP[0:8, 2:3], POH[:, 4:5])
                env.se.tick(inst)

                nc.tensor.wait_ge(env.ve.sem, env.ve.n)
                nc.tensor.matmul(PWB[:, 0:1], sb["ONESF"][0:1, 0:128], R[:, 55:56], start=True, stop=True)
                inst = nc.tensor.matmul(PWB[:, 1:2], sb["ONESF"][0:1, 0:128], R[:, 56:57], start=True, stop=True)
                env.te.tick(inst)
                nc.scalar.wait_ge(env.te.sem, env.te.n)
                inst = nc.scalar.copy(WEC[:, :], PWB[:, :])
                env.se.tick(inst)

                nc.tensor.wait_ge(env.se.sem, env.se.n)
                nc.tensor.matmul(PWB[:, 0:1], sb["ONESF"][0:1, 0:128], R[:, 50:51], start=True, stop=True)
                inst = nc.tensor.matmul(PWB[:, 1:2], sb["ONESF"][0:1, 0:128], R[:, 51:52], start=True, stop=True)
                env.te.tick(inst)
                nc.scalar.wait_ge(env.te.sem, env.te.n)
                nc.scalar.drain()
                inst = nc.scalar.copy(EBC[:, :], PWB[:, :])
                env.se.tick(inst)

                nc.tensor.wait_ge(env.se.sem, env.se.n)
                nc.tensor.wait_ge(env.qg.sem, pre_qg)
                for slot in range(2):
                    for fc in range(NFC):
                        nc.tensor.wait_ge(env.se.sem, env.se.n)
                        inst = nc.tensor.matmul(PB1[:, slot:slot + 1],
                                                B1SB[:, fc * 128:(fc + 1) * 128],
                                                OHP[:, slot:slot + 1], start=True, stop=True)
                        env.te.tick(inst)
                        nc.scalar.wait_ge(env.te.sem, env.te.n)
                        inst = nc.scalar.copy(B1C[:, slot * NFC + fc: slot * NFC + fc + 1],
                                              PB1[:, slot:slot + 1])
                        env.se.tick(inst)
                for dc in range(ND):
                    nc.tensor.wait_ge(env.se.sem, env.se.n)
                    inst = nc.tensor.matmul(PB1[:, 0:1], B2SB[:, dc * 128:(dc + 1) * 128],
                                            OHP[:, 2:3], start=True, stop=True)
                    env.te.tick(inst)
                    nc.scalar.wait_ge(env.te.sem, env.te.n)
                    inst = nc.scalar.copy(B2C[:, dc:dc + 1], PB1[:, 0:1])
                    env.se.tick(inst)

                inst = nc.gpsimd.iota(IOTU[:, :], pattern=[[0, 1]], base=0, channel_multiplier=1)
                env.ge.tick(inst)
                nc.vector.wait_ge(env.ge.sem, env.ge.n)
                nc.vector.tensor_copy(IOTC[:, 0:1], IOTU[:, :])
                nc.vector.drain()
                nc.vector.tensor_scalar(IOTC[:, 1:2], IOTC[:, 0:1], 2.0, None, op0=ALU.mult)
                nc.vector.drain()
                nc.vector.wait_ge(env.se.sem, env.se.n)
                ic = 0
                idxmap = {}
                for slot in range(2):
                    for dc in range(ND):
                        for half in range(2):
                            nc.vector.tensor_scalar(IDXF[:, ic:ic + 1], EBC[:, slot:slot + 1],
                                                    2048.0, float(dc * 256 + half),
                                                    op0=ALU.mult, op1=ALU.add)
                            idxmap[("w1", slot, dc, half)] = ic
                            ic += 1
                for slot in range(2):
                    for rc in range(NFC):
                        nc.vector.tensor_scalar(IDXF[:, ic:ic + 1], EBC[:, slot:slot + 1],
                                                4096.0, float(rc * 128), op0=ALU.mult, op1=ALU.add)
                        idxmap[("w2", slot, rc)] = ic
                        ic += 1
                nc.vector.drain()
                for slot in range(2):
                    for dc in range(ND):
                        for half in range(2):
                            c = idxmap[("w1", slot, dc, half)]
                            nc.vector.tensor_tensor(IDXF[:, c:c + 1], IDXF[:, c:c + 1],
                                                    IOTC[:, 1:2], op=ALU.add)
                for slot in range(2):
                    for rc in range(NFC):
                        c = idxmap[("w2", slot, rc)]
                        nc.vector.tensor_tensor(IDXF[:, c:c + 1], IDXF[:, c:c + 1],
                                                IOTC[:, 0:1], op=ALU.add)
                inst = nc.vector.nop()
                nc.vector.drain()
                env.ve.tick(nc.vector.tensor_copy(IOTC[:, 0:1], IOTC[:, 0:1]))
                nc.gpsimd.wait_ge(env.ve.sem, env.ve.n)
                for c in range(ic):
                    nc.gpsimd.tensor_copy(IDX[:, c:c + 1], IDXF[:, c:c + 1])
                inst = nc.gpsimd.engine_nop() if hasattr(nc.gpsimd, "engine_nop") else nc.gpsimd.nop()
                env.ge.tick(inst)

                w1view = dram["w0f1"].rearrange("r (h c) -> (r h) c", c=2048)
                gb = [GT0, GT1]
                gi = 0
                ve_scale = env.ve.n
                for slot in range(2):
                    s_w1 = dram["s_w1a"] if slot == 0 else dram["s_w1b"]
                    for dc in range(ND):
                        for half in range(2):
                            g = gb[gi % 2]
                            c = idxmap[("w1", slot, dc, half)]
                            nc.gpsimd.indirect_dma_start(
                                g[:, :], None, w1view,
                                bass.IndirectOffsetOnAxis(ap=IDX[:, c:c + 1], axis=0)
                            ).then_inc(env.qg.sem, 16)
                            env.qg.n += 16
                            nc.gpsimd.wait_ge(env.qg.sem, env.qg.n)
                            env.ld_g(s_w1[dc * 128:(dc + 1) * 128, half * 2048:(half + 1) * 2048], g[:, :])
                            nc.gpsimd.wait_ge(env.qg.sem, env.qg.n)
                            gi += 1
                for slot in range(2):
                    s_w2 = dram["s_w2a"] if slot == 0 else dram["s_w2b"]
                    for rc in range(NFC):
                        g = gb[gi % 2]
                        c = idxmap[("w2", slot, rc)]
                        nc.gpsimd.indirect_dma_start(
                            g[:, 0:1024], None, dram["w0f2"],
                            bass.IndirectOffsetOnAxis(ap=IDX[:, c:c + 1], axis=0)
                        ).then_inc(env.qg.sem, 16)
                        env.qg.n += 16
                        nc.vector.wait_ge(env.qg.sem, env.qg.n)
                        inst = nc.vector.tensor_scalar(g[:, 0:1024], g[:, 0:1024],
                                                       WEC[:, slot:slot + 1], None, op0=ALU.mult)
                        env.ve.tick(inst)
                        nc.gpsimd.wait_ge(env.ve.sem, env.ve.n)
                        env.ld_g(s_w2[rc * 128:(rc + 1) * 128, :], g[:, 0:1024])
                        nc.gpsimd.wait_ge(env.qg.sem, env.qg.n)
                        gi += 1
                env.barrier()

        with ExitStack() as sQ:
            XLNQ = sQ.enter_context(nc.sbuf_tensor(n("f_xlnq"), [128, ND * TQ], BF16))
            G = sQ.enter_context(nc.sbuf_tensor(n("f_g"), [128, (2 if not l1f else 1) * NFC * TQ], BF16))
            W1B = sQ.enter_context(nc.sbuf_tensor(n("f_w1b"), [128, 2 * NSL * ND * 128], BF16))
            W2B = sQ.enter_context(nc.sbuf_tensor(n("f_w2b"), [128, 2 * (2 if not l1f else 1) * NFC * 128], BF16))
            HP = sQ.enter_context(nc.psum_tensor(n("f_hp"), [128, (4 if not l1f else 2) * TQ], F32))
            YP = sQ.enter_context(nc.psum_tensor(n("f_yp"), [128, 2 * TQ], F32))
            if l1f:
                MBSB = sQ.enter_context(nc.sbuf_tensor(n("f_mbsb"), [128, TQ], BF16))
                YF = sQ.enter_context(nc.sbuf_tensor(n("f_yf"), [128, ND * TQ], F32))
            else:
                YPS = sQ.enter_context(nc.sbuf_tensor(n("f_yps"), [128, 2 * TQ], F32))

            w1src = ([dram["s_w1a"], dram["s_w1b"]] if not l1f
                     else [dram["s_w1x"][e * D:(e + 1) * D, :] for e in range(E)])
            w2src = [dram["s_w2a"], dram["s_w2b"]] if not l1f else [dram["s_wo"]]
            NW2 = len(w2src)

            def load_w1(fc, buf):
                base = buf * NSL * ND * 128
                for sl in range(NSL):
                    dst = W1B[:, base + sl * ND * 128: base + (sl + 1) * ND * 128]
                    dst3 = dst.rearrange("p (dc f) -> p dc f", f=128)
                    src3 = w1src[sl].rearrange("(dc p) f -> p dc f", p=128)[:, :, fc * 128:(fc + 1) * 128]
                    inst = nc.scalar.dma_start(dst3, src3)
                    env.qa.tick(inst, 16)
                return env.qa.n

            def load_w2(dc, buf):
                base = buf * NW2 * NFC * 128
                for sl in range(NW2):
                    dst = W2B[:, base + sl * NFC * 128: base + (sl + 1) * NFC * 128]
                    dst3 = dst.rearrange("p (fc d) -> p fc d", d=128)
                    src3 = w2src[sl].rearrange("(fc p) d -> p fc d", p=128)[:, :, dc * 128:(dc + 1) * 128]
                    inst = nc.sync.dma_start(dst3, src3)
                    env.qs.tick(inst, 16)
                return env.qs.n

            env.fence_dma(nc.scalar)
            env.fence_dma(nc.sync)
            for q in range(NQ):
                qscope = ExitStack()
                env.fence_dma(nc.gpsimd)
                for dc in range(ND):
                    env.ld_g(XLNQ[:, dc * TQ:(dc + 1) * TQ],
                             dram["s_xln"][dc * 128:(dc + 1) * 128, q * TQ:(q + 1) * TQ])
                xq_loaded = env.qg.n

                if l1f:
                    XM = qscope.enter_context(nc.sbuf_tensor(n("f_xm"), [128, E * ND * TQ], BF16))
                    nc.tensor.wait_ge(env.qg.sem, xq_loaded)
                    nc.tensor.wait_ge(env.se.sem, env.se.n)
                    for e in range(E):
                        inst = nc.tensor.matmul(YP[:, 0:TQ], sb["E8"][:, e * 128:(e + 1) * 128],
                                                MASKTP[:, q * TQ:(q + 1) * TQ], start=True, stop=True)
                        env.te.tick(inst)
                        nc.scalar.wait_ge(env.te.sem, env.te.n)
                        inst = nc.scalar.copy(MBSB[:, :], YP[:, 0:TQ])
                        env.se.tick(inst)
                        nc.vector.wait_ge(env.se.sem, env.se.n)
                        for dc in range(ND):
                            inst = nc.vector.tensor_tensor(
                                XM[:, (e * ND + dc) * TQ:(e * ND + dc + 1) * TQ],
                                XLNQ[:, dc * TQ:(dc + 1) * TQ], MBSB[:, :], op=ALU.mult)
                        env.ve.tick(inst)
                        nc.tensor.wait_ge(env.ve.sem, env.ve.n)

                se_g_counts = {}
                te_h_counts = {}
                ld_counts = {}
                ld_counts[0] = load_w1(0, 0)
                for fc in range(NFC):
                    if fc + 1 < NFC:
                        if fc >= 1:
                            nc.scalar.wait_ge(env.te.sem, te_h_counts.get(fc - 1, 0))
                        ld_counts[fc + 1] = load_w1(fc + 1, (fc + 1) % 2)
                    nc.tensor.wait_ge(env.qa.sem, ld_counts[fc])
                    if not l1f:
                        nc.tensor.wait_ge(env.qg.sem, xq_loaded)
                    base = (fc % 2) * NSL * ND * 128
                    if l1f:
                        hb = HP[:, (fc % 2) * TQ:(fc % 2 + 1) * TQ]
                        nc.tensor.wait_ge(env.se.sem, se_g_counts.get(fc - 2, 0))
                        for sl in range(NSL):
                            for dc in range(ND):
                                nc.tensor.matmul(hb,
                                                 W1B[:, base + (sl * ND + dc) * 128: base + (sl * ND + dc + 1) * 128],
                                                 XM[:, (sl * ND + dc) * TQ:(sl * ND + dc + 1) * TQ],
                                                 start=(sl == 0 and dc == 0), stop=False)
                        inst = nc.tensor.matmul(hb, B1SB[:, fc * 128:(fc + 1) * 128],
                                                MASKTP[:, q * TQ:(q + 1) * TQ],
                                                start=False, stop=True)
                        env.te.tick(inst)
                        te_h_counts[fc] = env.te.n
                        nc.scalar.wait_ge(env.te.sem, env.te.n)
                        inst = nc.scalar.activation(G[:, fc * TQ:(fc + 1) * TQ], hb, AF.Gelu,
                                                    bias=sb["CSC"][:, 2:3])
                        env.se.tick(inst)
                        se_g_counts[fc] = env.se.n
                    else:
                        nc.tensor.wait_ge(env.se.sem, se_g_counts.get(fc - 2, 0))
                        for sl in range(2):
                            hb = HP[:, ((fc % 2) * 2 + sl) * TQ:((fc % 2) * 2 + sl + 1) * TQ]
                            for dc in range(ND):
                                inst = nc.tensor.matmul(
                                    hb,
                                    W1B[:, base + (sl * ND + dc) * 128: base + (sl * ND + dc + 1) * 128],
                                    XLNQ[:, dc * TQ:(dc + 1) * TQ],
                                    start=(dc == 0), stop=(dc == ND - 1))
                            env.te.tick(inst)
                            te_h_counts[fc] = env.te.n
                            nc.scalar.wait_ge(env.te.sem, env.te.n)
                            inst = nc.scalar.activation(
                                G[:, (sl * NFC + fc) * TQ:(sl * NFC + fc + 1) * TQ], hb, AF.Gelu,
                                bias=B1C[:, sl * NFC + fc: sl * NFC + fc + 1])
                            env.se.tick(inst)
                        se_g_counts[fc] = env.se.n
                g_done_se = env.se.n

                yld = {}
                yld[0] = load_w2(0, 0)
                te_y_counts = {}
                se_y_counts = {}
                for dc in range(ND):
                    if dc + 1 < ND:
                        if dc >= 1:
                            nc.sync.wait_ge(env.te.sem, te_y_counts.get(dc - 1, 0))
                        yld[dc + 1] = load_w2(dc + 1, (dc + 1) % 2)
                    nc.tensor.wait_ge(env.qs.sem, yld[dc])
                    nc.tensor.wait_ge(env.se.sem, max(g_done_se, se_y_counts.get(dc - 2, 0)))
                    yb = YP[:, (dc % 2) * TQ:(dc % 2 + 1) * TQ]
                    base = (dc % 2) * NW2 * NFC * 128
                    first = True
                    for sl in range(NW2):
                        for fc in range(NFC):
                            gsl = G[:, (sl * NFC + fc) * TQ:(sl * NFC + fc + 1) * TQ]
                            inst = nc.tensor.matmul(
                                yb,
                                W2B[:, base + (sl * NFC + fc) * 128: base + (sl * NFC + fc + 1) * 128],
                                gsl, start=first, stop=(sl == NW2 - 1 and fc == NFC - 1))
                            first = False
                    env.te.tick(inst)
                    te_y_counts[dc] = env.te.n
                    nc.scalar.wait_ge(env.te.sem, env.te.n)
                    if l1f:
                        inst = nc.scalar.activation(YF[:, dc * TQ:(dc + 1) * TQ], yb, AF.Identity,
                                                    bias=BOC[:, dc:dc + 1])
                        env.se.tick(inst)
                        se_y_counts[dc] = env.se.n
                        nc.vector.wait_ge(env.se.sem, env.se.n)
                        inst = nc.vector.tensor_tensor(YF[:, dc * TQ:(dc + 1) * TQ],
                                                       YF[:, dc * TQ:(dc + 1) * TQ],
                                                       XLNQ[:, dc * TQ:(dc + 1) * TQ], op=ALU.add)
                        env.ve.tick(inst)
                        nc.tensor.wait_ge(env.ve.sem, env.ve.n)
                    else:
                        if dc >= 2:
                            nc.scalar.wait_ge(env.qg.sem, env.qg.n)
                        inst = nc.scalar.activation(YPS[:, (dc % 2) * TQ:(dc % 2 + 1) * TQ], yb,
                                                    AF.Identity, bias=B2C[:, dc:dc + 1])
                        env.se.tick(inst)
                        se_y_counts[dc] = env.se.n
                        nc.gpsimd.wait_ge(env.se.sem, env.se.n)
                        nc.gpsimd.wait_ge(env.qs.sem, rs_stored)
                        env.ld_g(rs_dst[dc * 128:(dc + 1) * 128, q * TQ:(q + 1) * TQ],
                                 YPS[:, (dc % 2) * TQ:(dc % 2 + 1) * TQ], accum_op=ALU.add)

                if l1f and dbg is not None and q == int(dbg.split(":")[1]) and dbg.startswith("dbgf:"):
                    qscope.close()
                    nc.gpsimd.wait_ge(env.ve.sem, env.ve.n)
                    nc.gpsimd.wait_ge(env.se.sem, env.se.n)
                    for dc in range(ND):
                        env.ld_g(dram["outT"][dc * 128:(dc + 1) * 128, 0:TQ], YF[:, dc * TQ:(dc + 1) * TQ])
                    for fc in range(3):
                        env.ld_g(dram["outT"][0:128, TQ + fc * TQ: TQ + (fc + 1) * TQ], G[:, fc * TQ:(fc + 1) * TQ])
                    nc.sync.wait_ge(env.qg.sem, env.qg.n)
                    env.barrier()
                    return
                if l1f:
                    nc.gpsimd.wait_ge(env.te.sem, env.te.n)
                    qscope.close()
                    ln2scope = ExitStack()
                    ln2 = _alloc_ln(env, ln2scope, want_out=True)
                    _emit_ln(env, ln2,
                             lambda dc: YF[:, dc * TQ:(dc + 1) * TQ],
                             None, eps_col=3,
                             gamma=lambda dc: GAC[:, dc:dc + 1],
                             beta=lambda dc: BEC[:, dc:dc + 1],
                             out_f32=lambda dc: ln2["XSQ"].bitcast(F32)[:, dc * TQ // 2:(dc + 1) * TQ // 2]
                             if False else ln2["XSQ2"][:, dc * TQ:(dc + 1) * TQ])
                    nc.gpsimd.wait_ge(env.ve.sem, env.ve.n)
                    nc.gpsimd.wait_ge(env.qs.sem, rs_stored)
                    for dc in range(ND):
                        env.ld_g(rs_dst[dc * 128:(dc + 1) * 128, q * TQ:(q + 1) * TQ],
                                 ln2["XSQ2"][:, dc * TQ:(dc + 1) * TQ], accum_op=ALU.add)
                    nc.vector.wait_ge(env.qg.sem, env.qg.n)
                    nc.tensor.wait_ge(env.qg.sem, env.qg.n)
                    nc.gpsimd.wait_ge(env.ve.sem, env.ve.n)
                    if dbg is not None and dbg.startswith("dbgx:") and q == int(dbg.split(":")[1]):
                        for dc in range(ND):
                            env.ld_g(dram["outT"][dc * 128:(dc + 1) * 128, TQ:2 * TQ],
                                     ln2["XSQ2"][:, dc * TQ:(dc + 1) * TQ])
                        nc.sync.wait_ge(env.qg.sem, env.qg.n)
                        env.barrier()
                        ln2scope.close()
                        return
                    ln2scope.close()
                else:
                    qscope.close()
                env.barrier()

            env.barrier()


_SHAPES = {}


def _declare_dram(nc):
    dram = {}
    for k, shp in _SHAPES.items():
        dram[k] = nc.dram_tensor(k, list(shp), F32, kind="ExternalInput").ap()
    dram["xT"] = nc.dram_tensor("xT", [D, T], F32, kind="ExternalInput").ap()
    dram["rsd"] = nc.dram_tensor("rsd", [D, T], F32, kind="Internal").ap()
    dram["s_w1a"] = nc.dram_tensor("s_w1a", [D, FF], BF16, kind="Internal").ap()
    dram["s_w1b"] = nc.dram_tensor("s_w1b", [D, FF], BF16, kind="Internal").ap()
    dram["s_w2a"] = nc.dram_tensor("s_w2a", [FF, D], BF16, kind="Internal").ap()
    dram["s_w2b"] = nc.dram_tensor("s_w2b", [FF, D], BF16, kind="Internal").ap()
    dram["s_w1x"] = nc.dram_tensor("s_w1x", [E * D, FF], BF16, kind="Internal").ap()
    dram["s_wo"] = nc.dram_tensor("s_wo", [FF, D], BF16, kind="Internal").ap()
    dram["s_xln"] = nc.dram_tensor("s_xln", [D, T], BF16, kind="Internal").ap()
    dram["outT"] = nc.dram_tensor("outT", [D, T], F32, kind="ExternalOutput").ap()
    return dram


def _build_program(blocks):
    nc = bass.Bass("TRN2", target_bir_lowering=False)
    with ExitStack() as ctx:
        env = Env(nc, ctx)
        env.dram = _declare_dram(nc)
        block = ctx.enter_context(nc.Block())
        _load_consts(env)
        cur = "xT"
        for b in blocks:
            if b == "dump_xln":
                with ExitStack() as sA:
                    RS = sA.enter_context(nc.sbuf_tensor("d_rs", [128, ND * T], F32))
                    env.fence_dma(nc.sync)
                    for dc in range(ND):
                        env.ld_s(RS[:, dc * T:(dc + 1) * T], env.dram[cur][dc * 128:(dc + 1) * 128, :])
                    rs_loaded = env.qs.n
                    ln_sc = _alloc_ln(env, sA)
                    nc.gpsimd.wait_ge(env.qs.sem, rs_loaded)
                    nc.gpsimd.wait_ge(env.qg.sem, env.consts_qg)
                    nc.tensor.wait_ge(env.qs.sem, env.consts_qs)
                    nc.tensor.wait_ge(env.qg.sem, env.consts_qg)
                    for q in range(NQ):
                        _emit_ln(env, ln_sc,
                                 lambda dc, q=q: RS[:, dc * T + q * TQ: dc * T + (q + 1) * TQ],
                                 lambda dc, q=q: env.sb["XLN"][:, dc * T + q * TQ: dc * T + (q + 1) * TQ])
                    nc.gpsimd.wait_ge(env.ve.sem, env.ve.n)
                    for dc in range(ND):
                        env.ld_g(env.dram["outT"][dc * 128:(dc + 1) * 128, :],
                                 env.sb["XLN"][:, dc * T:(dc + 1) * T])
                    nc.sync.wait_ge(env.qg.sem, env.qg.n)
                nc = nc
                cur = "outT"
                continue
            if b.startswith("dbg0a"):
                _emit_attn(env, "w0q", "w0k", "w0v", "w0p", "b0p", rope=False,
                           rs_src=env.dram[cur], rs_dst=env.dram["rsd"], dbg=b)
                cur = "outT"
            elif b == "l0a":
                _emit_attn(env, "w0q", "w0k", "w0v", "w0p", "b0p", rope=False,
                           rs_src=env.dram[cur], rs_dst=env.dram["rsd"])
                cur = "rsd"
            elif b == "l1a":
                _emit_attn(env, "w1q", "w1k", "w1v", "w1p", "b1p", rope=True,
                           rs_src=env.dram[cur], rs_dst=env.dram["rsd"])
                cur = "rsd"
            elif b == "l0f":
                _emit_ffn(env, False, rs_src=env.dram[cur], rs_dst=env.dram["rsd"])
                cur = "rsd"
            elif b == "l1f":
                _emit_ffn(env, True, rs_src=env.dram[cur], rs_dst=env.dram["outT"])
                cur = "outT"
            elif b.startswith("dbgf") or b.startswith("dbgx"):
                _emit_ffn(env, True, rs_src=env.dram[cur], rs_dst=env.dram["outT"], dbg=b)
                cur = "outT"
            else:
                raise NotImplementedError(b)
        env.fence_dma(nc.sync)
        if cur != "outT":
            env.ld_s(env.dram["outT"][:, :], env.dram[cur][:, :])
        nc.sync.wait_ge(env.qs.sem, env.qs.n)
    return nc


def _run_chain(inputs, blocks=ALL_BLOCKS, n_cores=None):
    global _SHAPES
    shared, xs = _host_prep(inputs)
    if n_cores is not None:
        xs = xs[:n_cores]
    _SHAPES = {k: v.shape for k, v in shared.items()}
    nc = _build_program(tuple(blocks))
    in_maps = []
    for b in range(len(xs)):
        m = dict(shared)
        m["xT"] = xs[b]
        in_maps.append(m)
    res = run_bass_kernel_spmd(nc, in_maps, core_ids=list(range(len(xs))))
    return np.stack([np.ascontiguousarray(r["outT"].T) for r in res.results])


def kernel(**inputs):
    return _run_chain(inputs, ALL_BLOCKS)


# version 02
# speedup vs baseline: 1.0707x; 1.0707x over previous
import numpy as np
from contextlib import ExitStack

import concourse.bass as bass
import concourse.mybir as mybir
from concourse.bass_utils import run_bass_kernel_spmd

F32 = mybir.dt.float32
BF16 = mybir.dt.bfloat16
U32 = mybir.dt.uint32
ALU = mybir.AluOpType
AF = mybir.ActivationFunctionType
AX = mybir.AxisListType

T = 2048
D = 1024
H = 16
DH = 64
FF = 4096
E = 8
ND = D // 128
NQ = 4
TQ = T // NQ
NT = T // 128
NFC = FF // 128
CEXP = 12.0
LN_EPS = 1e-5
LN_EPS_L1F = 1e-12

ALL_BLOCKS = ("l0a", "l0f", "l1a", "l1f")


class _Cnt:
    def __init__(self, sem):
        self.sem = sem
        self.n = 0

    def tick(self, inst, k=1):
        inst.then_inc(self.sem, k)
        self.n += k
        return self.n


class Env:
    def __init__(self, nc, ctx):
        self.nc = nc
        self.ctx = ctx
        self.uid = 0
        self.te = _Cnt(ctx.enter_context(nc.semaphore("s_te")))
        self.ve = _Cnt(ctx.enter_context(nc.semaphore("s_ve")))
        self.se = _Cnt(ctx.enter_context(nc.semaphore("s_se")))
        self.ge = _Cnt(ctx.enter_context(nc.semaphore("s_ge")))
        self.qs = _Cnt(ctx.enter_context(nc.semaphore("s_qs")))
        self.qa = _Cnt(ctx.enter_context(nc.semaphore("s_qa")))
        self.qg = _Cnt(ctx.enter_context(nc.semaphore("s_qg")))
        self.sb = {}

    def name(self, base):
        self.uid += 1
        return f"{base}_{self.uid}"

    def ld_s(self, out, in_):
        return self.qs.tick(self.nc.sync.dma_start(out, in_), 16)

    def ld_a(self, out, in_):
        return self.qa.tick(self.nc.scalar.dma_start(out, in_), 16)

    def ld_g(self, out, in_, **kw):
        return self.qg.tick(self.nc.gpsimd.dma_start(out, in_, **kw), 16)

    def fence_dma(self, eng):
        for c in (self.te, self.ve, self.se, self.ge, self.qs, self.qa, self.qg):
            eng.wait_ge(c.sem, c.n)

    def barrier(self):
        nc = self.nc
        for eng in (nc.tensor, nc.vector, nc.scalar, nc.gpsimd, nc.sync):
            for c in (self.te, self.ve, self.se, self.ge, self.qs, self.qa, self.qg):
                eng.wait_ge(c.sem, c.n)


def _host_prep(inputs):
    d = {}
    f32 = np.float32

    wkqv = np.asarray(inputs["L0_attn_w_kqv"], f32).reshape(H, DH, 3, D)
    d["w0q"] = np.ascontiguousarray(wkqv[:, :, 1, :].reshape(H * DH, D).T)
    d["w0k"] = np.ascontiguousarray(wkqv[:, :, 0, :].reshape(H * DH, D).T)
    d["w0v"] = np.ascontiguousarray(wkqv[:, :, 2, :].reshape(H * DH, D).T)
    d["w0p"] = np.ascontiguousarray(np.asarray(inputs["L0_attn_w_proj"], f32).T)
    d["b0p"] = np.ascontiguousarray(np.asarray(inputs["L0_attn_b_proj"], f32).reshape(ND, 128).T)

    d["w0r"] = np.ascontiguousarray(np.asarray(inputs["L0_ffn_Wr"], f32))
    d["w0f1"] = np.ascontiguousarray(np.asarray(inputs["L0_ffn_W1"], f32).reshape(E * D, FF))
    d["w0f2"] = np.ascontiguousarray(np.asarray(inputs["L0_ffn_W2"], f32).reshape(E * FF, D))
    b0f1 = np.zeros((128, FF), f32)
    b0f1[:E] = np.asarray(inputs["L0_ffn_b1"], f32)
    d["b0f1"] = b0f1
    b0f2 = np.zeros((128, D), f32)
    b0f2[:E] = np.asarray(inputs["L0_ffn_b2"], f32)
    d["b0f2"] = b0f2

    perm = np.concatenate([np.arange(0, DH, 2), np.arange(1, DH, 2)])
    wq = np.asarray(inputs["L1_attn_wq"], f32).reshape(D, H, DH)[:, :, perm].reshape(D, D)
    wk = np.asarray(inputs["L1_attn_wk"], f32).reshape(D, H, DH)[:, :, perm].reshape(D, D)
    d["w1q"] = np.ascontiguousarray(wq)
    d["w1k"] = np.ascontiguousarray(wk)
    d["w1v"] = np.ascontiguousarray(np.asarray(inputs["L1_attn_wv"], f32))
    d["w1p"] = np.ascontiguousarray(np.asarray(inputs["L1_attn_wo_w"], f32))
    d["b1p"] = np.ascontiguousarray(np.asarray(inputs["L1_attn_wo_b"], f32).reshape(ND, 128).T)

    d["w1r"] = np.ascontiguousarray(np.asarray(inputs["L1_ffn_Wr"], f32).T)
    d["w1f1"] = np.ascontiguousarray(np.asarray(inputs["L1_ffn_W1"], f32))
    b1f1 = np.zeros((128, FF), f32)
    b1f1[:E] = np.asarray(inputs["L1_ffn_b1"], f32)
    d["b1f1"] = b1f1
    d["w1fo"] = np.ascontiguousarray(np.asarray(inputs["L1_ffn_Wo"], f32))
    d["b1fo"] = np.ascontiguousarray(np.asarray(inputs["L1_ffn_bo"], f32).reshape(ND, 128).T)
    d["g1f"] = np.ascontiguousarray(np.asarray(inputs["L1_ffn_gamma"], f32).reshape(ND, 128).T)
    d["be1f"] = np.ascontiguousarray(np.asarray(inputs["L1_ffn_beta"], f32).reshape(ND, 128).T)

    d["c_ones"] = np.ones((1, TQ), f32)
    d["c_ocol"] = np.ones((128, 1), f32)
    d["c_ident"] = np.eye(128, dtype=f32)
    d["c_iota8"] = np.arange(8, dtype=f32)[None, :]
    masks = np.zeros((128, 4 * TQ), f32)
    negm = np.zeros((128, 4 * TQ), f32)
    i = np.arange(128)[:, None]
    j = np.arange(TQ)[None, :]
    for oo in range(4):
        allowed = (j <= i + 128 * oo)
        masks[:, oo * TQ:(oo + 1) * TQ] = allowed.astype(f32)
        negm[:, oo * TQ:(oo + 1) * TQ] = np.where(allowed, 0.0, -1e30).astype(f32)
    d["c_masks"] = masks
    d["c_negm"] = negm
    csc = np.zeros((128, 4), f32)
    csc[:, 0] = -CEXP
    csc[:, 1] = LN_EPS
    csc[:, 3] = LN_EPS_L1F
    d["c_csc"] = csc
    inv = 10000.0 ** (-np.arange(32, dtype=np.float64) * 2.0 / DH)
    ang = np.outer(np.arange(T, dtype=np.float64), inv)
    cosv = np.cos(ang).T.astype(f32)
    sinv = np.sin(ang).T.astype(f32)
    ropc = np.zeros((128, T), f32)
    rops = np.zeros((128, T), f32)
    for g in range(4):
        ropc[g * 32:(g + 1) * 32] = cosv
        rops[g * 32:(g + 1) * 32] = sinv * (-1.0 if (g % 2 == 0) else 1.0)
    d["c_ropc"] = ropc
    d["c_rops"] = rops
    e8 = np.zeros((128, 8 * 128), f32)
    for e in range(8):
        e8[e, e * 128:(e + 1) * 128] = 1.0
    d["c_e8"] = e8

    x = np.asarray(inputs["L0_attn_x"], f32)
    xs = [np.ascontiguousarray(x[b].T) for b in range(x.shape[0])]
    return d, xs


def _load_consts(env):
    nc, ctx, sb, dram = env.nc, env.ctx, env.sb, env.dram
    sb["ONES"] = ctx.enter_context(nc.sbuf_tensor("c_ones_sb", [1, TQ], BF16))
    sb["OCOL"] = ctx.enter_context(nc.sbuf_tensor("c_ocol_sb", [128, 1], BF16))
    sb["IDENT"] = ctx.enter_context(nc.sbuf_tensor("c_ident_sb", [128, 128], BF16))
    sb["IOTA8"] = ctx.enter_context(nc.sbuf_tensor("c_iota8_sb", [1, 8], F32))
    sb["NEGM"] = ctx.enter_context(nc.sbuf_tensor("c_negm_sb", [128, 4 * TQ], F32))
    sb["CSC"] = ctx.enter_context(nc.sbuf_tensor("c_csc_sb", [128, 4], F32))
    sb["E8"] = ctx.enter_context(nc.sbuf_tensor("c_e8_sb", [128, 8 * 128], BF16))
    sb["ONESF"] = ctx.enter_context(nc.sbuf_tensor("c_onesf_sb", [1, TQ], F32))
    sb["IDENTF"] = ctx.enter_context(nc.sbuf_tensor("c_identf_sb", [128, 128], F32))
    sb["OCOLF"] = ctx.enter_context(nc.sbuf_tensor("c_ocolf_sb", [128, 1], F32))

    env.ld_g(sb["ONES"][:, :], dram["c_ones"][:, :])
    env.ld_g(sb["OCOL"][:, :], dram["c_ocol"][:, :])
    env.ld_g(sb["IDENT"][:, :], dram["c_ident"][:, :])
    env.ld_s(sb["IOTA8"][:, :], dram["c_iota8"][:, :])
    env.ld_g(sb["NEGM"][:, :], dram["c_negm"][:, :])
    env.ld_s(sb["CSC"][:, :], dram["c_csc"][:, :])
    env.ld_g(sb["E8"][:, :], dram["c_e8"][:, :])
    env.ld_s(sb["ONESF"][:, :], dram["c_ones"][:, :])
    env.ld_s(sb["IDENTF"][:, :], dram["c_ident"][:, :])
    env.ld_s(sb["OCOLF"][:, :], dram["c_ocol"][:, :])
    env.consts_qg = env.qg.n
    env.consts_qs = env.qs.n


def _alloc_ln(env, scope, want_out=False):
    nc = env.nc
    n = env.name
    sc = {}
    sc["XB"] = scope.enter_context(nc.sbuf_tensor(n("ln_xb"), [128, ND * TQ], BF16))
    sc["XSQ"] = scope.enter_context(nc.sbuf_tensor(n("ln_xsq"), [128, ND * TQ], BF16))
    sc["RM"] = scope.enter_context(nc.sbuf_tensor(n("ln_rm"), [1, TQ], F32))
    sc["RT"] = scope.enter_context(nc.sbuf_tensor(n("ln_rt"), [1, TQ], F32))
    sc["RV"] = scope.enter_context(nc.sbuf_tensor(n("ln_rv"), [1, TQ], F32))

    sc["RRb"] = scope.enter_context(nc.sbuf_tensor(n("ln_rrb"), [1, TQ], BF16))
    sc["RAb"] = scope.enter_context(nc.sbuf_tensor(n("ln_rab"), [1, TQ], BF16))
    sc["RRf"] = scope.enter_context(nc.sbuf_tensor(n("ln_rrf"), [1, TQ], F32))
    sc["RAf"] = scope.enter_context(nc.sbuf_tensor(n("ln_raf"), [1, TQ], F32))
    sc["RBS"] = scope.enter_context(nc.sbuf_tensor(n("ln_rbs"), [128, TQ], BF16))
    sc["ABS"] = scope.enter_context(nc.sbuf_tensor(n("ln_abs"), [128, TQ], BF16))
    if want_out:
        sc["XSQ2"] = scope.enter_context(nc.sbuf_tensor(n("ln_xsq2"), [128, ND * TQ], F32))
    sc["PS1"] = scope.enter_context(nc.psum_tensor(n("ln_ps1"), [1, TQ], F32))
    sc["PS2"] = scope.enter_context(nc.psum_tensor(n("ln_ps2"), [1, TQ], F32))
    sc["PRB"] = scope.enter_context(nc.psum_tensor(n("ln_prb"), [128, TQ], F32))
    sc["PAB"] = scope.enter_context(nc.psum_tensor(n("ln_pab"), [128, TQ], F32))
    return sc


def _emit_ln(env, ln_sc, rs_ap, dst_xln, eps_col=1, gamma=None, beta=None, out_f32=None):
    nc, sb = env.nc, env.sb
    XB, XSQ = ln_sc["XB"], ln_sc["XSQ"]
    RM, RT, RV, RRb, RAb = (ln_sc[k] for k in ("RM", "RT", "RV", "RRb", "RAb"))
    RL = RV

    PS1, PS2, PRB, PAB = (ln_sc[k] for k in ("PS1", "PS2", "PRB", "PAB"))
    RBS, ABS = ln_sc["RBS"], ln_sc["ABS"]

    nc.gpsimd.wait_ge(env.ve.sem, env.ve.n)
    for dc in range(ND):
        nc.gpsimd.tensor_copy(XB[:, dc * TQ:(dc + 1) * TQ], rs_ap(dc))
    for dc in range(ND):
        inst = nc.gpsimd.tensor_tensor(XSQ[:, dc * TQ:(dc + 1) * TQ], XB[:, dc * TQ:(dc + 1) * TQ],
                                       XB[:, dc * TQ:(dc + 1) * TQ], op=ALU.mult)
    env.ge.tick(inst)

    nc.tensor.wait_ge(env.ge.sem, env.ge.n)
    nc.tensor.wait_ge(env.ve.sem, env.ve.n)
    for dc in range(ND):
        nc.tensor.matmul(PS1[:, :], sb["OCOL"][:, :], XB[:, dc * TQ:(dc + 1) * TQ],
                         start=(dc == 0), stop=(dc == ND - 1))
    for dc in range(ND):
        inst = nc.tensor.matmul(PS2[:, :], sb["OCOL"][:, :], XSQ[:, dc * TQ:(dc + 1) * TQ],
                                start=(dc == 0), stop=(dc == ND - 1))
    env.te.tick(inst)

    nc.vector.wait_ge(env.te.sem, env.te.n)
    nc.vector.tensor_scalar(RM[:, :], PS1[:, :], 1.0 / D, None, op0=ALU.mult)
    nc.vector.tensor_scalar(RT[:, :], PS2[:, :], 1.0 / D, None, op0=ALU.mult)
    nc.vector.drain()
    nc.vector.tensor_tensor(RV[:, :], RM[:, :], RM[:, :], op=ALU.mult)
    nc.vector.drain()
    inst = nc.vector.tensor_tensor(RV[:, :], RT[:, :], RV[:, :], op=ALU.subtract)
    env.ve.tick(inst)

    RRf, RAf = ln_sc["RRf"], ln_sc["RAf"]
    nc.scalar.wait_ge(env.ve.sem, env.ve.n)
    nc.scalar.activation(RL[:, :], RV[:, :], AF.Ln, bias=sb["CSC"][0:1, eps_col:eps_col + 1])
    nc.scalar.drain()
    nc.scalar.activation(RRf[:, :], RL[:, :], AF.Exp, scale=-0.5)
    nc.scalar.drain()
    inst = nc.scalar.copy(RRb[:, :], RRf[:, :])
    env.se.tick(inst)

    nc.vector.wait_ge(env.se.sem, env.se.n)
    nc.vector.scalar_tensor_tensor(RAf[:, :], RM[:, :], -1.0, RRf[:, :], op0=ALU.mult, op1=ALU.mult)
    nc.vector.drain()
    inst = nc.vector.tensor_copy(RAb[:, :], RAf[:, :])
    env.ve.tick(inst)

    nc.tensor.wait_ge(env.ve.sem, env.ve.n)
    nc.tensor.wait_ge(env.se.sem, env.se.n)
    nc.tensor.matmul(PRB[:, :], sb["ONES"][0:1, 0:128], RRb[:, :], start=True, stop=True)
    inst = nc.tensor.matmul(PAB[:, :], sb["ONES"][0:1, 0:128], RAb[:, :], start=True, stop=True)
    env.te.tick(inst)

    nc.scalar.wait_ge(env.te.sem, env.te.n)
    nc.scalar.copy(RBS[:, :], PRB[:, :])
    inst = nc.scalar.copy(ABS[:, :], PAB[:, :])
    env.se.tick(inst)

    nc.vector.wait_ge(env.se.sem, env.se.n)
    for dc in range(ND):
        nc.vector.tensor_tensor(XB[:, dc * TQ:(dc + 1) * TQ], XB[:, dc * TQ:(dc + 1) * TQ],
                                RBS[:, :], op=ALU.mult)
    if gamma is None:
        for dc in range(ND):
            inst = nc.vector.tensor_tensor(dst_xln(dc), XB[:, dc * TQ:(dc + 1) * TQ], ABS[:, :], op=ALU.add)
    else:
        for dc in range(ND):
            nc.vector.tensor_tensor(XB[:, dc * TQ:(dc + 1) * TQ], XB[:, dc * TQ:(dc + 1) * TQ],
                                    ABS[:, :], op=ALU.add)
        nc.vector.drain()
        for dc in range(ND):
            inst = nc.vector.tensor_scalar(out_f32(dc), XB[:, dc * TQ:(dc + 1) * TQ],
                                           gamma(dc), beta(dc), op0=ALU.mult, op1=ALU.add)
    env.ve.tick(inst)


def _emit_attn(env, wq_key, wk_key, wv_key, wp_key, bp_key, rope, rs_src, rs_dst, dbg=None):
    nc, sb, dram = env.nc, env.sb, env.dram
    n = env.name
    NV = H * DH

    with ExitStack() as outer:
        KF = outer.enter_context(nc.sbuf_tensor(n("a_kf"), [128, ND * T], BF16))
        VT = outer.enter_context(nc.sbuf_tensor(n("a_vt"), [128, NT * NV], BF16))
        if rope:
            ROPC = outer.enter_context(nc.sbuf_tensor(n("a_ropc"), [128, T], BF16))
            ROPS = outer.enter_context(nc.sbuf_tensor(n("a_rops"), [128, T], BF16))
            RSWP = outer.enter_context(nc.sbuf_tensor(n("a_rswp"), [128, T // 2], BF16))
            RTMP = outer.enter_context(nc.sbuf_tensor(n("a_rtmp"), [128, T // 2], BF16))
        sAB = outer.enter_context(ExitStack())
        XLN = sAB.enter_context(nc.sbuf_tensor(n("a_xln"), [128, ND * T], BF16))
        sb = dict(sb, XLN=XLN)

        with ExitStack() as sA:
            RSQ = sA.enter_context(nc.sbuf_tensor(n("a_rsq"), [128, ND * TQ], F32))
            ln_sc = _alloc_ln(env, sA)
            nc.gpsimd.wait_ge(env.qg.sem, env.consts_qg)
            nc.tensor.wait_ge(env.qs.sem, env.consts_qs)
            nc.tensor.wait_ge(env.qg.sem, env.consts_qg)
            for q in range(NQ):
                env.fence_dma(nc.sync)
                nc.sync.wait_ge(env.ge.sem, env.ge.n)
                for dc in range(ND):
                    env.ld_s(RSQ[:, dc * TQ:(dc + 1) * TQ],
                             rs_src[dc * 128:(dc + 1) * 128, q * TQ:(q + 1) * TQ])
                rs_q_loaded = env.qs.n
                if rs_dst is not rs_src:
                    for dc in range(ND):
                        env.ld_s(rs_dst[dc * 128:(dc + 1) * 128, q * TQ:(q + 1) * TQ],
                                 RSQ[:, dc * TQ:(dc + 1) * TQ])
                nc.gpsimd.wait_ge(env.qs.sem, rs_q_loaded)
                _emit_ln(env, ln_sc,
                         lambda dc: RSQ[:, dc * TQ:(dc + 1) * TQ],
                         lambda dc, q=q: sb["XLN"][:, dc * T + q * TQ: dc * T + (q + 1) * TQ])
                nc.scalar.wait_ge(env.ve.sem, env.ve.n)
                for dc in range(ND):
                    inst = nc.scalar.dma_start(
                        dram["s_xln"][dc * 128:(dc + 1) * 128, q * TQ:(q + 1) * TQ],
                        sb["XLN"][:, dc * T + q * TQ: dc * T + (q + 1) * TQ])
                    env.qa.tick(inst, 16)
            xln_ve = env.ve.n
            xln_stored = env.qa.n
            rs_stored = env.qs.n
            nc.sync.wait_ge(env.ge.sem, env.ge.n)

        with ExitStack() as sBC:
            if rope:
                env.fence_dma(nc.gpsimd)
                env.ld_g(ROPC[:, :], dram["c_ropc"][:, :])
                env.ld_g(ROPS[:, :], dram["c_rops"][:, :])
                rope_loaded = env.qg.n

            with ExitStack() as sB:
                WK = sB.enter_context(nc.sbuf_tensor(n("a_wk"), [128, ND * D], BF16))
                WV = sB.enter_context(nc.sbuf_tensor(n("a_wv"), [128, ND * NV], BF16))
                PK0 = sB.enter_context(nc.psum_tensor(n("a_pk0"), [128, TQ], F32))
                PK1 = sB.enter_context(nc.psum_tensor(n("a_pk1"), [128, TQ], F32))
                PV0 = sB.enter_context(nc.psum_tensor(n("a_pv0"), [128, NV], F32))
                PV1 = sB.enter_context(nc.psum_tensor(n("a_pv1"), [128, NV], F32))
                env.fence_dma(nc.gpsimd)
                for dc in range(ND):
                    env.ld_g(WK[:, dc * D:(dc + 1) * D], dram[wk_key][dc * 128:(dc + 1) * 128, :])
                for dc in range(ND):
                    env.ld_g(WV[:, dc * NV:(dc + 1) * NV], dram[wv_key][dc * 128:(dc + 1) * 128, :])
                wkv_loaded = env.qg.n

                nc.tensor.wait_ge(env.qg.sem, wkv_loaded)
                nc.tensor.wait_ge(env.ve.sem, xln_ve)
                se_base = env.se.n
                fill = 0
                for jc in range(ND):
                    for q in range(NQ):
                        pk = PK0 if (fill % 2 == 0) else PK1
                        if fill >= 2:
                            nc.tensor.wait_ge(env.se.sem, se_base + fill - 1)
                        for dc in range(ND):
                            inst = nc.tensor.matmul(pk[:, :],
                                                    WK[:, dc * D + jc * 128: dc * D + (jc + 1) * 128],
                                                    sb["XLN"][:, dc * T + q * TQ: dc * T + (q + 1) * TQ],
                                                    start=(dc == 0), stop=(dc == ND - 1))
                        env.te.tick(inst)
                        nc.scalar.wait_ge(env.te.sem, env.te.n)
                        inst = nc.scalar.copy(KF[:, jc * T + q * TQ: jc * T + (q + 1) * TQ], pk[:, :])
                        env.se.tick(inst)
                        fill += 1
                k_se = env.se.n

                if rope:
                    nc.vector.wait_ge(env.se.sem, k_se)
                    nc.vector.wait_ge(env.qg.sem, rope_loaded)
                    TH = T // 2
                    for jc in range(ND):
                        for hf in range(2):
                            kf = KF[:, jc * T + hf * TH: jc * T + (hf + 1) * TH]
                            for g in range(4):
                                sp = (g // 2) * 64 + (1 - (g % 2)) * 32
                                nc.vector.tensor_copy(RSWP[g * 32:(g + 1) * 32, 0:TH],
                                                      KF[sp:sp + 32, jc * T + hf * TH: jc * T + (hf + 1) * TH])
                            nc.vector.drain()
                            nc.vector.tensor_tensor(RTMP[:, 0:TH], kf,
                                                    ROPC[:, hf * TH:(hf + 1) * TH], op=ALU.mult)
                            nc.vector.tensor_tensor(RSWP[:, 0:TH], RSWP[:, 0:TH],
                                                    ROPS[:, hf * TH:(hf + 1) * TH], op=ALU.mult)
                            nc.vector.drain()
                            inst = nc.vector.tensor_tensor(kf, RTMP[:, 0:TH], RSWP[:, 0:TH], op=ALU.add)
                            env.ve.tick(inst)
                    nc.tensor.wait_ge(env.ve.sem, env.ve.n)

                se_base_v = env.se.n
                for tc in range(NT):
                    pv = PV0 if (tc % 2 == 0) else PV1
                    if tc >= 2:
                        nc.tensor.wait_ge(env.se.sem, se_base_v + tc - 1)
                    for (c0, c1) in ((0, 512), (512, 1024)):
                        for dc in range(ND):
                            inst = nc.tensor.matmul(pv[:, c0:c1],
                                                    sb["XLN"][:, dc * T + tc * 128: dc * T + (tc + 1) * 128],
                                                    WV[:, dc * NV + c0: dc * NV + c1],
                                                    start=(dc == 0), stop=(dc == ND - 1))
                    env.te.tick(inst)
                    nc.scalar.wait_ge(env.te.sem, env.te.n)
                    inst = nc.scalar.copy(VT[:, tc * NV:(tc + 1) * NV], pv[:, :])
                    env.se.tick(inst)

            env.barrier()
            sAB.close()
            with ExitStack() as sC:
                XLNQ = sC.enter_context(nc.sbuf_tensor(n("a_xlnq"), [128, ND * TQ], BF16))
                WQ = sC.enter_context(nc.sbuf_tensor(n("a_wq"), [128, ND * D], BF16))
                WP = sC.enter_context(nc.sbuf_tensor(n("a_wp"), [128, ND * D], BF16))
                BP = sC.enter_context(nc.sbuf_tensor(n("a_bp"), [128, ND], F32))
                QF = sC.enter_context(nc.sbuf_tensor(n("a_qf"), [128, ND * TQ], BF16))
                ATT = sC.enter_context(nc.sbuf_tensor(n("a_att"), [128, NT * TQ], BF16))
                AQ = sC.enter_context(nc.sbuf_tensor(n("a_aq"), [128, 3 * T], BF16))
                SCR = sC.enter_context(nc.sbuf_tensor(n("a_scr"), [128, 2 * T], F32))
                CTXF = sC.enter_context(nc.sbuf_tensor(n("a_ctxf"), [128, ND * TQ], BF16))
                PRJ = sC.enter_context(nc.sbuf_tensor(n("a_prj"), [128, 2 * TQ], F32))
                MC = sC.enter_context(nc.sbuf_tensor(n("a_mc"), [128, 10], F32))
                PSC = sC.enter_context(nc.psum_tensor(n("a_psc"), [128, T], F32))
                PT0 = sC.enter_context(nc.psum_tensor(n("a_pt0"), [128, 1024], BF16))
                PT1 = sC.enter_context(nc.psum_tensor(n("a_pt1"), [128, 1024], BF16))
                PC0 = sC.enter_context(nc.psum_tensor(n("a_pc0"), [64, TQ], F32))
                PC1 = sC.enter_context(nc.psum_tensor(n("a_pc1"), [64, TQ], F32))

                env.fence_dma(nc.gpsimd)
                for dc in range(ND):
                    env.ld_g(WQ[:, dc * D:(dc + 1) * D], dram[wq_key][dc * 128:(dc + 1) * 128, :])
                for dc in range(ND):
                    env.ld_g(WP[:, dc * D:(dc + 1) * D], dram[wp_key][dc * 128:(dc + 1) * 128, :])
                wqp_loaded = env.qg.n
                env.fence_dma(nc.sync)
                env.ld_s(BP[:, :], dram[bp_key][:, :])
                bp_loaded = env.qs.n

                nc.tensor.wait_ge(env.qg.sem, wqp_loaded)
                qc_range = range(NQ) if dbg is None else range(int(dbg.split(":")[1]), int(dbg.split(":")[1]) + 1)
                for qc in qc_range:
                    kmax = 4 * qc + 4
                    kn = qc + 1
                    kw = kn * TQ
                    nc.gpsimd.wait_ge(env.te.sem, env.te.n)
                    nc.gpsimd.wait_ge(env.qa.sem, xln_stored)
                    for dc in range(ND):
                        env.ld_g(XLNQ[:, dc * TQ:(dc + 1) * TQ],
                                 dram["s_xln"][dc * 128:(dc + 1) * 128, qc * TQ:(qc + 1) * TQ])
                    xlnq_loaded = env.qg.n
                    nc.tensor.wait_ge(env.se.sem, env.se.n)
                    nc.tensor.wait_ge(env.ve.sem, env.ve.n)
                    nc.tensor.wait_ge(env.qg.sem, xlnq_loaded)
                    se_base_q = env.se.n
                    for jcq in range(ND):
                        pq = PSC[:, 0:TQ] if (jcq % 2 == 0) else PSC[:, TQ:2 * TQ]
                        if jcq >= 2:
                            nc.tensor.wait_ge(env.se.sem, se_base_q + jcq - 1)
                        for dc in range(ND):
                            inst = nc.tensor.matmul(pq,
                                                    WQ[:, dc * D + jcq * 128: dc * D + (jcq + 1) * 128],
                                                    XLNQ[:, dc * TQ:(dc + 1) * TQ],
                                                    start=(dc == 0), stop=(dc == ND - 1))
                        env.te.tick(inst)
                        nc.scalar.wait_ge(env.te.sem, env.te.n)
                        inst = nc.scalar.copy(QF[:, jcq * TQ:(jcq + 1) * TQ], pq)
                        env.se.tick(inst)
                    q_se = env.se.n

                    if rope:
                        nc.vector.wait_ge(env.se.sem, q_se)
                        for jcq in range(ND):
                            qf = QF[:, jcq * TQ:(jcq + 1) * TQ]
                            for g in range(4):
                                sp = (g // 2) * 64 + (1 - (g % 2)) * 32
                                nc.vector.tensor_copy(RSWP[g * 32:(g + 1) * 32, 0:TQ],
                                                      QF[sp:sp + 32, jcq * TQ:(jcq + 1) * TQ])
                            nc.vector.drain()
                            nc.vector.tensor_tensor(RTMP[:, 0:TQ], qf,
                                                    ROPC[:, qc * TQ:(qc + 1) * TQ], op=ALU.mult)
                            nc.vector.tensor_tensor(RSWP[:, 0:TQ], RSWP[:, 0:TQ],
                                                    ROPS[:, qc * TQ:(qc + 1) * TQ], op=ALU.mult)
                            nc.vector.drain()
                            inst = nc.vector.tensor_tensor(qf, RTMP[:, 0:TQ], RSWP[:, 0:TQ], op=ALU.add)
                            env.ve.tick(inst)

                    nc.tensor.wait_ge(env.se.sem, q_se)
                    if rope:
                        nc.tensor.wait_ge(env.ve.sem, env.ve.n)

                    NU = H * 4
                    te_transp = {}
                    se_exp = {}
                    ve_bias = {}
                    ve_norm = {}
                    se_trevac = {}
                    te_ctx = {}
                    se_ctxev = {}
                    bank_free = [env.ve.n] * 4
                    psc_head = 0

                    def unit_hql(u):
                        return u // 4, u % 4

                    pt_free = [0, 0]
                    pt_idx = [0]

                    def emit_transposes(u):
                        h, ql = unit_hql(u)
                        for grp in range((kmax + 3) // 4):
                            par = pt_idx[0] % 2
                            pt_idx[0] += 1
                            ptb = PT0 if par == 0 else PT1
                            nt_ = min(kmax, grp * 4 + 4) - grp * 4
                            nc.tensor.wait_ge(env.ve.sem, pt_free[par])
                            nc.tensor.wait_ge(env.ve.sem, ve_norm[u])
                            for ti in range(nt_):
                                kc = grp * 4 + ti
                                inst = nc.tensor.transpose(ptb[:, ti * 128:(ti + 1) * 128],
                                                           AQ[:, (u % 3) * T + kc * 128:(u % 3) * T + (kc + 1) * 128],
                                                           sb["IDENT"][:, :])
                            env.te.tick(inst)
                            te_transp[u] = env.te.n
                            nc.scalar.wait_ge(env.te.sem, env.te.n)
                            if h >= 1:
                                nc.scalar.wait_ge(env.te.sem, te_ctx.get(h - 1, 0))
                            dst = ATT[:, :].rearrange("p (k q) -> p k q", q=TQ)[
                                :, grp * 4: grp * 4 + nt_, ql * 128:(ql + 1) * 128]
                            nc.vector.wait_ge(env.te.sem, env.te.n)
                            if h >= 1:
                                nc.vector.wait_ge(env.te.sem, te_ctx.get(h - 1, 0))
                            inst = nc.vector.tensor_copy(dst, ptb[:, 0:nt_ * 128])
                            env.ve.tick(inst)
                            pt_free[par] = env.ve.n
                        se_trevac[u] = env.ve.n

                    def emit_ctx(h):
                        nc.tensor.wait_ge(env.ve.sem, se_trevac[h * 4 + 3])
                        nc.tensor.wait_ge(env.se.sem, se_ctxev.get(h - 2, 0))
                        pc = PC0 if (h % 2 == 0) else PC1
                        jc = h // 2
                        pp = (h % 2) * 64
                        for kc in range(kmax):
                            inst = nc.tensor.matmul(pc[0:64, :],
                                                    VT[:, kc * NV + h * DH: kc * NV + (h + 1) * DH],
                                                    ATT[:, kc * TQ:(kc + 1) * TQ],
                                                    start=(kc == 0), stop=(kc == kmax - 1))
                        env.te.tick(inst)
                        te_ctx[h] = env.te.n
                        nc.scalar.wait_ge(env.te.sem, env.te.n)
                        inst = nc.scalar.copy(CTXF[pp:pp + 64, jc * TQ:(jc + 1) * TQ], pc[0:64, :])
                        env.se.tick(inst)
                        se_ctxev[h] = env.se.n

                    for u in range(NU):
                        h, ql = unit_hql(u)
                        jc = h // 2
                        pp = (h % 2) * 64
                        scr = SCR[:, (u % 2) * T:(u % 2) * T + T]
                        aq = AQ[:, (u % 3) * T:(u % 3) * T + T]
                        banks = [(psc_head + i) % 4 for i in range(kn)]
                        psc_head = (psc_head + kn) % 4
                        for b in banks:
                            nc.tensor.wait_ge(env.ve.sem, bank_free[b])
                        for i, kk in enumerate(range(kn)):
                            b = banks[i]
                            inst = nc.tensor.matmul(
                                PSC[:, b * TQ:(b + 1) * TQ],
                                QF[pp:pp + 64, jc * TQ + ql * 128: jc * TQ + (ql + 1) * 128],
                                KF[pp:pp + 64, jc * T + kk * TQ: jc * T + (kk + 1) * TQ],
                                start=True, stop=True)
                        env.te.tick(inst)
                        te_scores = env.te.n
                        nc.vector.wait_ge(env.te.sem, te_scores)
                        if u >= 2:
                            nc.vector.wait_ge(env.se.sem, se_exp[u - 2])
                        for i, kk in enumerate(range(kn)):
                            b = banks[i]
                            if kk == kn - 1:
                                inst = nc.vector.scalar_tensor_tensor(
                                    scr[:, kk * TQ:(kk + 1) * TQ], PSC[:, b * TQ:(b + 1) * TQ], 1.0,
                                    sb["NEGM"][:, ql * TQ:(ql + 1) * TQ], op0=ALU.mult, op1=ALU.add)
                            else:
                                inst = nc.vector.tensor_scalar(
                                    scr[:, kk * TQ:(kk + 1) * TQ], PSC[:, b * TQ:(b + 1) * TQ],
                                    1.0, None, op0=ALU.mult, op1=ALU.max, accum_out=MC[:, kk:kk + 1])
                            env.ve.tick(inst)
                            bank_free[b] = env.ve.n
                        nc.vector.drain()
                        inst = nc.vector.reduce_max(MC[:, 3:4], scr[:, (kn - 1) * TQ:kn * TQ], axis=AX.X)
                        env.ve.tick(inst)
                        nc.vector.drain()
                        for kk in range(kn - 1):
                            inst = nc.vector.tensor_tensor(MC[:, 3:4], MC[:, 3:4], MC[:, kk:kk + 1], op=ALU.max)
                            env.ve.tick(inst)
                            nc.vector.drain()
                        bcol = 4 + (u % 2)
                        inst = nc.vector.tensor_scalar(MC[:, bcol:bcol + 1], MC[:, 3:4], -0.125, None, op0=ALU.mult)
                        env.ve.tick(inst)
                        ve_bias[u] = env.ve.n
                        nc.scalar.wait_ge(env.ve.sem, ve_bias[u])
                        if u >= 3:
                            nc.scalar.wait_ge(env.te.sem, te_transp.get(u - 3, 0))
                        inst = nc.scalar.activation(aq[:, 0:kn * TQ], scr[:, 0:kn * TQ], AF.Exp,
                                                    scale=0.125, bias=MC[:, 4 + (u % 2):5 + (u % 2)],
                                                    accum_out=MC[:, 6 + (u % 2):7 + (u % 2)])
                        env.se.tick(inst)
                        se_exp[u] = env.se.n
                        nc.vector.wait_ge(env.se.sem, se_exp[u])
                        nc.vector.reciprocal(MC[:, 8 + (u % 2):9 + (u % 2)], MC[:, 6 + (u % 2):7 + (u % 2)])
                        nc.vector.drain()
                        inst = nc.vector.tensor_scalar(aq[:, 0:kn * TQ], aq[:, 0:kn * TQ],
                                                       MC[:, 8 + (u % 2):9 + (u % 2)], None, op0=ALU.mult)
                        env.ve.tick(inst)
                        ve_norm[u] = env.ve.n
                        if u >= 1:
                            emit_transposes(u - 1)
                        if ql == 0 and h >= 1:
                            emit_ctx(h - 1)
                    emit_transposes(NU - 1)
                    emit_ctx(H - 1)
                    ctx_se = env.se.n

                    nc.tensor.wait_ge(env.se.sem, ctx_se)
                    nc.tensor.wait_ge(env.ve.sem, env.ve.n)
                    nc.scalar.wait_ge(env.qs.sem, bp_loaded)
                    nc.gpsimd.wait_ge(env.qs.sem, rs_stored)
                    se_base_p = env.se.n
                    qg_base_p = env.qg.n
                    for jcp in range(ND):
                        pq = PSC[:, 0:TQ] if (jcp % 2 == 0) else PSC[:, TQ:2 * TQ]
                        if jcp >= 2:
                            nc.tensor.wait_ge(env.se.sem, se_base_p + jcp - 1)
                        for cc in range(ND):
                            inst = nc.tensor.matmul(pq,
                                                    WP[:, cc * D + jcp * 128: cc * D + (jcp + 1) * 128],
                                                    CTXF[:, cc * TQ:(cc + 1) * TQ],
                                                    start=(cc == 0), stop=(cc == ND - 1))
                        env.te.tick(inst)
                        nc.scalar.wait_ge(env.te.sem, env.te.n)
                        if jcp >= 2:
                            nc.scalar.wait_ge(env.qg.sem, qg_base_p + 16 * (jcp - 1))
                        inst = nc.scalar.activation(PRJ[:, (jcp % 2) * TQ:(jcp % 2 + 1) * TQ], pq,
                                                    AF.Identity, bias=BP[:, jcp:jcp + 1])
                        env.se.tick(inst)
                        nc.gpsimd.wait_ge(env.se.sem, env.se.n)
                        env.ld_g(rs_dst[jcp * 128:(jcp + 1) * 128, qc * TQ:(qc + 1) * TQ],
                                 PRJ[:, (jcp % 2) * TQ:(jcp % 2 + 1) * TQ], accum_op=ALU.add)
                    nc.scalar.wait_ge(env.qg.sem, env.qg.n)

                env.barrier()



def _emit_ffn(env, l1f, rs_src, rs_dst, dbg=None):
    nc, sb, dram = env.nc, env.sb, env.dram
    n = env.name
    NSL = 8 if l1f else 2

    with ExitStack() as outer:
        if l1f:
            MASKTP = outer.enter_context(nc.sbuf_tensor(n("f_masktp"), [128, T], BF16))
            WRT = outer.enter_context(nc.sbuf_tensor(n("f_wrt"), [128, ND * 8], F32))
            CSUM = outer.enter_context(nc.sbuf_tensor(n("f_csum"), [1, 8], F32))
            RSTC = outer.enter_context(nc.sbuf_tensor(n("f_rstc"), [128, 1], F32))
            LGT = outer.enter_context(nc.sbuf_tensor(n("f_lgt"), [128, 8], F32))
            RMX = outer.enter_context(nc.sbuf_tensor(n("f_rmx"), [128, 1], F32))
            MTOK = outer.enter_context(nc.sbuf_tensor(n("f_mtok"), [128, 8], BF16))
            B1SB = outer.enter_context(nc.sbuf_tensor(n("f_b1sb"), [128, FF], BF16))
            BOC = outer.enter_context(nc.sbuf_tensor(n("f_boc"), [128, ND], F32))
            GAC = outer.enter_context(nc.sbuf_tensor(n("f_gac"), [128, ND], F32))
            BEC = outer.enter_context(nc.sbuf_tensor(n("f_bec"), [128, ND], F32))
        else:
            PLS = outer.enter_context(nc.sbuf_tensor(n("f_pls"), [128, ND * NQ], F32))
            DUMP = outer.enter_context(nc.sbuf_tensor(n("f_dump"), [128, TQ], BF16))
            WRR = outer.enter_context(nc.sbuf_tensor(n("f_wrr"), [128, ND * 8], F32))
            B1SB = outer.enter_context(nc.sbuf_tensor(n("f_b1sb"), [128, FF], BF16))
            B2SB = outer.enter_context(nc.sbuf_tensor(n("f_b2sb"), [128, D], BF16))
            B1C = outer.enter_context(nc.sbuf_tensor(n("f_b1c"), [128, 64], F32))
            B2C = outer.enter_context(nc.sbuf_tensor(n("f_b2c"), [128, ND], F32))
            WEC = outer.enter_context(nc.sbuf_tensor(n("f_wec"), [128, 2], F32))
            EBC = outer.enter_context(nc.sbuf_tensor(n("f_ebc"), [128, 2], F32))

        env.fence_dma(nc.sync)
        if l1f:
            for dc in range(ND):
                env.ld_s(WRT[:, dc * 8:(dc + 1) * 8], dram["w1r"][dc * 128:(dc + 1) * 128, :])
            env.fence_dma(nc.gpsimd)
            env.ld_g(B1SB[:, :], dram["b1f1"][:, :])
            env.ld_s(BOC[:, :], dram["b1fo"][:, :])
            env.ld_s(GAC[:, :], dram["g1f"][:, :])
            env.ld_s(BEC[:, :], dram["be1f"][:, :])
        else:
            for dc in range(ND):
                env.ld_s(WRR[:, dc * 8:(dc + 1) * 8], dram["w0r"][dc * 128:(dc + 1) * 128, :])
            env.fence_dma(nc.gpsimd)
            env.ld_g(B1SB[:, :], dram["b0f1"][:, :])
            env.ld_g(B2SB[:, :], dram["b0f2"][:, :])
        pre_qs = env.qs.n
        pre_qg = env.qg.n

        with ExitStack() as sA:
            RS = sA.enter_context(nc.sbuf_tensor(n("f_rs"), [128, ND * T], F32))
            XBUF = sA.enter_context(nc.sbuf_tensor(n("f_xbuf"), [128, ND * TQ], BF16))
            env.fence_dma(nc.sync)
            for dc in range(ND):
                env.ld_s(RS[:, dc * T:(dc + 1) * T], rs_src[dc * 128:(dc + 1) * 128, :])
            rs_loaded = env.qs.n
            ln_sc = _alloc_ln(env, sA)
            if l1f:
                PR = sA.enter_context(nc.psum_tensor(n("f_pr"), [128, 8], F32))
                PTR = sA.enter_context(nc.psum_tensor(n("f_ptr"), [128, 1], F32))
                PMT = sA.enter_context(nc.psum_tensor(n("f_pmt"), [8, 128], BF16))
                nc.vector.wait_ge(env.qg.sem, env.consts_qg)
                inst = nc.vector.memset(MASKTP[:, :], 0.0)
                env.ve.tick(inst)
                nc.tensor.wait_ge(env.qs.sem, pre_qs)
                nc.tensor.wait_ge(env.qg.sem, env.consts_qg)
                for dc in range(ND):
                    inst = nc.tensor.matmul(PMT[0:1, 0:8] if False else PR[0:1, 0:8],
                                            sb["OCOLF"][:, :], WRT[:, dc * 8:(dc + 1) * 8],
                                            start=(dc == 0), stop=(dc == ND - 1))
                env.te.tick(inst)
                nc.scalar.wait_ge(env.te.sem, env.te.n)
                inst = nc.scalar.copy(CSUM[:, :], PR[0:1, 0:8])
                env.se.tick(inst)
            nc.gpsimd.wait_ge(env.qs.sem, rs_loaded)
            nc.gpsimd.wait_ge(env.qg.sem, env.consts_qg)
            nc.tensor.wait_ge(env.qs.sem, env.consts_qs)
            nc.tensor.wait_ge(env.qg.sem, env.consts_qg)
            for q in range(NQ):
                _emit_ln(env, ln_sc,
                         lambda dc, q=q: RS[:, dc * T + q * TQ: dc * T + (q + 1) * TQ],
                         lambda dc: XBUF[:, dc * TQ:(dc + 1) * TQ])
                xq_ve = env.ve.n
                nc.scalar.wait_ge(env.ve.sem, xq_ve)
                for dc in range(ND):
                    inst = nc.scalar.dma_start(dram["s_xln"][dc * 128:(dc + 1) * 128, q * TQ:(q + 1) * TQ],
                                               XBUF[:, dc * TQ:(dc + 1) * TQ])
                    env.qa.tick(inst, 16)
                xq_stored = env.qa.n
                nc.vector.wait_ge(env.qa.sem, xq_stored)
                if not l1f:
                    nc.scalar.wait_ge(env.ve.sem, xq_ve)
                    for dc in range(ND):
                        inst = nc.scalar.activation(DUMP[:, :], XBUF[:, dc * TQ:(dc + 1) * TQ],
                                                    AF.Identity,
                                                    accum_out=PLS[:, dc * NQ + q: dc * NQ + q + 1])
                        env.se.tick(inst)
                    nc.vector.wait_ge(env.se.sem, env.se.n)
                else:
                    RRf, RAf = ln_sc["RRf"], ln_sc["RAf"]
                    for tc in range(4):
                        tg = q * 4 + tc
                        nc.tensor.wait_ge(env.se.sem, env.se.n)
                        nc.tensor.wait_ge(env.ve.sem, env.ve.n)
                        for dc in range(ND):
                            nc.tensor.matmul(PR[:, :],
                                             RS[:, dc * T + tg * 128: dc * T + (tg + 1) * 128],
                                             WRT[:, dc * 8:(dc + 1) * 8],
                                             start=(dc == 0), stop=False)
                        nc.tensor.matmul(PR[:, :], RAf[0:1, tc * 128:(tc + 1) * 128], CSUM[:, :],
                                         start=False, stop=True)
                        inst = nc.tensor.transpose(PTR[:, :], RRf[0:1, tc * 128:(tc + 1) * 128],
                                                   sb["IDENTF"][0:1, 0:1])
                        env.te.tick(inst)
                        nc.scalar.wait_ge(env.te.sem, env.te.n)
                        inst = nc.scalar.copy(RSTC[:, :], PTR[:, :])
                        env.se.tick(inst)
                        nc.vector.wait_ge(env.se.sem, env.se.n)
                        nc.vector.tensor_scalar(LGT[:, :], PR[:, :], RSTC[:, :], None, op0=ALU.mult)
                        nc.vector.drain()
                        nc.vector.reduce_max(RMX[:, :], LGT[:, :], axis=AX.X)
                        nc.vector.drain()
                        inst = nc.vector.tensor_scalar(MTOK[:, :], LGT[:, :], RMX[:, :], None, op0=ALU.is_ge)
                        env.ve.tick(inst)
                        nc.tensor.wait_ge(env.ve.sem, env.ve.n)
                        inst = nc.tensor.transpose(PMT[:, :], MTOK[:, :], sb["IDENT"][:, :])
                        env.te.tick(inst)
                        nc.scalar.wait_ge(env.te.sem, env.te.n)
                        nc.scalar.wait_ge(env.ve.sem, env.ve.n)
                        inst = nc.scalar.copy(MASKTP[0:8, tg * 128:(tg + 1) * 128], PMT[0:8, :])
                        env.se.tick(inst)
            if rs_dst is not rs_src:
                nc.sync.wait_ge(env.qs.sem, rs_loaded)
                for dc in range(ND):
                    env.ld_s(rs_dst[dc * 128:(dc + 1) * 128, :], RS[:, dc * T:(dc + 1) * T])
            rs_stored = env.qs.n
            nc.sync.wait_ge(env.ge.sem, env.ge.n)
            nc.sync.wait_ge(env.te.sem, env.te.n)

        if not l1f:
            with ExitStack() as sR:
                ROW8 = sR.enter_context(nc.sbuf_tensor(n("f_row8"), [1, 64], F32))
                ROWB = sR.enter_context(nc.sbuf_tensor(n("f_rowb"), [1, 24], BF16))
                PCOL = sR.enter_context(nc.sbuf_tensor(n("f_pcol"), [128, ND], F32))
                OHP = sR.enter_context(nc.sbuf_tensor(n("f_ohp"), [128, 3], BF16))
                IOTC = sR.enter_context(nc.sbuf_tensor(n("f_iotc"), [128, 2], F32))
                IOTU = sR.enter_context(nc.sbuf_tensor(n("f_iotu"), [128, 1], U32))
                IDXF = sR.enter_context(nc.sbuf_tensor(n("f_idxf"), [128, 96], F32))
                IDX = sR.enter_context(nc.sbuf_tensor(n("f_idx"), [128, 96], U32))
                GT0 = sR.enter_context(nc.sbuf_tensor(n("f_gt0"), [128, 2048], BF16))
                GT1 = sR.enter_context(nc.sbuf_tensor(n("f_gt1"), [128, 2048], BF16))
                PSL = sR.enter_context(nc.psum_tensor(n("f_psl"), [1, 64], F32))
                PB1 = sR.enter_context(nc.psum_tensor(n("f_pb1"), [128, 2], F32))
                POH = sR.enter_context(nc.psum_tensor(n("f_poh"), [8, 8], BF16))
                PWB = sR.enter_context(nc.psum_tensor(n("f_pwb"), [128, 2], F32))

                nc.vector.wait_ge(env.se.sem, env.se.n)
                for dc in range(ND):
                    nc.vector.tensor_tensor(PCOL[:, dc:dc + 1], PLS[:, dc * NQ:dc * NQ + 1],
                                            PLS[:, dc * NQ + 1:dc * NQ + 2], op=ALU.add)
                nc.vector.drain()
                for dc in range(ND):
                    nc.vector.tensor_tensor(IOTC[:, 0:1] if False else PCOL[:, dc:dc + 1],
                                            PCOL[:, dc:dc + 1], PLS[:, dc * NQ + 2:dc * NQ + 3], op=ALU.add)
                nc.vector.drain()
                for dc in range(ND):
                    nc.vector.tensor_tensor(PCOL[:, dc:dc + 1], PCOL[:, dc:dc + 1],
                                            PLS[:, dc * NQ + 3:dc * NQ + 4], op=ALU.add)
                nc.vector.drain()
                for dc in range(ND):
                    nc.vector.tensor_scalar(PCOL[:, dc:dc + 1], PCOL[:, dc:dc + 1], 1.0 / T, None, op0=ALU.mult)
                inst = nc.vector.nop()
                nc.vector.drain()
                env.ve.tick(nc.vector.tensor_copy(IOTC[:, 0:1], PCOL[:, 0:1]))

                nc.tensor.wait_ge(env.ve.sem, env.ve.n)
                nc.tensor.wait_ge(env.qs.sem, pre_qs)
                for dc in range(ND):
                    inst = nc.tensor.matmul(PSL[0:1, 0:8], PCOL[:, dc:dc + 1],
                                            WRR[:, dc * 8:(dc + 1) * 8],
                                            start=(dc == 0), stop=(dc == ND - 1))
                env.te.tick(inst)

                R = ROW8
                nc.vector.wait_ge(env.te.sem, env.te.n)
                nc.vector.tensor_copy(R[:, 0:8], PSL[0:1, 0:8])
                nc.vector.drain()
                nc.vector.reduce_max(R[:, 48:49], R[:, 0:8], axis=AX.X)
                nc.vector.drain()
                nc.vector.tensor_scalar(R[:, 8:16], R[:, 0:8], R[:, 48:49], None, op0=ALU.is_ge)
                nc.vector.drain()
                nc.vector.wait_ge(env.qs.sem, env.consts_qs)
                nc.vector.tensor_tensor(R[:, 32:40], R[:, 8:16], sb["IOTA8"][:, :], op=ALU.mult)
                nc.vector.drain()
                nc.vector.tensor_reduce(R[:, 50:51], R[:, 32:40], axis=AX.X, op=ALU.add)
                nc.vector.drain()
                nc.vector.scalar_tensor_tensor(R[:, 16:24], R[:, 8:16], -1e30, R[:, 0:8],
                                               op0=ALU.mult, op1=ALU.add)
                nc.vector.drain()
                nc.vector.reduce_max(R[:, 49:50], R[:, 16:24], axis=AX.X)
                nc.vector.drain()
                nc.vector.tensor_scalar(R[:, 24:32], R[:, 16:24], R[:, 49:50], None, op0=ALU.is_ge)
                nc.vector.drain()
                nc.vector.tensor_tensor(R[:, 32:40], R[:, 24:32], sb["IOTA8"][:, :], op=ALU.mult)
                nc.vector.drain()
                nc.vector.tensor_reduce(R[:, 51:52], R[:, 32:40], axis=AX.X, op=ALU.add)
                nc.vector.drain()
                inst = nc.vector.tensor_tensor(R[:, 52:53], R[:, 49:50], R[:, 48:49], op=ALU.subtract)
                env.ve.tick(inst)
                nc.scalar.wait_ge(env.ve.sem, env.ve.n)
                inst = nc.scalar.activation(R[:, 53:54], R[:, 52:53], AF.Exp)
                env.se.tick(inst)
                nc.vector.wait_ge(env.se.sem, env.se.n)
                nc.vector.tensor_scalar(R[:, 54:55], R[:, 53:54], 1.0, None, op0=ALU.add)
                nc.vector.drain()
                nc.vector.reciprocal(R[:, 55:56], R[:, 54:55])
                nc.vector.drain()
                nc.vector.tensor_scalar(R[:, 56:57], R[:, 55:56], -1.0, 1.0, op0=ALU.mult, op1=ALU.add)
                nc.vector.drain()
                nc.vector.tensor_scalar(R[:, 32:40], R[:, 8:16], R[:, 55:56], None, op0=ALU.mult)
                nc.vector.drain()
                nc.vector.tensor_scalar(R[:, 40:48], R[:, 24:32], R[:, 56:57], None, op0=ALU.mult)
                nc.vector.drain()
                nc.vector.tensor_tensor(R[:, 40:48], R[:, 40:48], R[:, 32:40], op=ALU.add)
                nc.vector.drain()
                nc.vector.tensor_copy(ROWB[:, 0:8], R[:, 8:16])
                nc.vector.tensor_copy(ROWB[:, 8:16], R[:, 24:32])
                inst = nc.vector.tensor_copy(ROWB[:, 16:24], R[:, 40:48])
                env.ve.tick(inst)

                nc.vector.drain()
                inst = nc.vector.memset(OHP[:, :], 0.0)
                env.ve.tick(inst)
                nc.tensor.wait_ge(env.ve.sem, env.ve.n)
                nc.tensor.transpose(POH[:, 0:1], ROWB[0:1, 0:8], sb["IDENT"][0:1, 0:1])
                nc.tensor.transpose(POH[:, 2:3], ROWB[0:1, 8:16], sb["IDENT"][0:1, 0:1])
                inst = nc.tensor.transpose(POH[:, 4:5], ROWB[0:1, 16:24], sb["IDENT"][0:1, 0:1])
                env.te.tick(inst)
                nc.scalar.wait_ge(env.te.sem, env.te.n)
                nc.scalar.copy(OHP[0:8, 0:1], POH[:, 0:1])
                nc.scalar.copy(OHP[0:8, 1:2], POH[:, 2:3])
                inst = nc.scalar.copy(OHP[0:8, 2:3], POH[:, 4:5])
                env.se.tick(inst)

                nc.tensor.wait_ge(env.ve.sem, env.ve.n)
                nc.tensor.matmul(PWB[:, 0:1], sb["ONESF"][0:1, 0:128], R[:, 55:56], start=True, stop=True)
                inst = nc.tensor.matmul(PWB[:, 1:2], sb["ONESF"][0:1, 0:128], R[:, 56:57], start=True, stop=True)
                env.te.tick(inst)
                nc.scalar.wait_ge(env.te.sem, env.te.n)
                inst = nc.scalar.copy(WEC[:, :], PWB[:, :])
                env.se.tick(inst)

                nc.tensor.wait_ge(env.se.sem, env.se.n)
                nc.tensor.matmul(PWB[:, 0:1], sb["ONESF"][0:1, 0:128], R[:, 50:51], start=True, stop=True)
                inst = nc.tensor.matmul(PWB[:, 1:2], sb["ONESF"][0:1, 0:128], R[:, 51:52], start=True, stop=True)
                env.te.tick(inst)
                nc.scalar.wait_ge(env.te.sem, env.te.n)
                nc.scalar.drain()
                inst = nc.scalar.copy(EBC[:, :], PWB[:, :])
                env.se.tick(inst)

                nc.tensor.wait_ge(env.se.sem, env.se.n)
                nc.tensor.wait_ge(env.qg.sem, pre_qg)
                for slot in range(2):
                    for fc in range(NFC):
                        nc.tensor.wait_ge(env.se.sem, env.se.n)
                        inst = nc.tensor.matmul(PB1[:, slot:slot + 1],
                                                B1SB[:, fc * 128:(fc + 1) * 128],
                                                OHP[:, slot:slot + 1], start=True, stop=True)
                        env.te.tick(inst)
                        nc.scalar.wait_ge(env.te.sem, env.te.n)
                        inst = nc.scalar.copy(B1C[:, slot * NFC + fc: slot * NFC + fc + 1],
                                              PB1[:, slot:slot + 1])
                        env.se.tick(inst)
                for dc in range(ND):
                    nc.tensor.wait_ge(env.se.sem, env.se.n)
                    inst = nc.tensor.matmul(PB1[:, 0:1], B2SB[:, dc * 128:(dc + 1) * 128],
                                            OHP[:, 2:3], start=True, stop=True)
                    env.te.tick(inst)
                    nc.scalar.wait_ge(env.te.sem, env.te.n)
                    inst = nc.scalar.copy(B2C[:, dc:dc + 1], PB1[:, 0:1])
                    env.se.tick(inst)

                inst = nc.gpsimd.iota(IOTU[:, :], pattern=[[0, 1]], base=0, channel_multiplier=1)
                env.ge.tick(inst)
                nc.vector.wait_ge(env.ge.sem, env.ge.n)
                nc.vector.tensor_copy(IOTC[:, 0:1], IOTU[:, :])
                nc.vector.drain()
                nc.vector.tensor_scalar(IOTC[:, 1:2], IOTC[:, 0:1], 2.0, None, op0=ALU.mult)
                nc.vector.drain()
                nc.vector.wait_ge(env.se.sem, env.se.n)
                ic = 0
                idxmap = {}
                for slot in range(2):
                    for dc in range(ND):
                        for half in range(2):
                            nc.vector.tensor_scalar(IDXF[:, ic:ic + 1], EBC[:, slot:slot + 1],
                                                    2048.0, float(dc * 256 + half),
                                                    op0=ALU.mult, op1=ALU.add)
                            idxmap[("w1", slot, dc, half)] = ic
                            ic += 1
                for slot in range(2):
                    for rc in range(NFC):
                        nc.vector.tensor_scalar(IDXF[:, ic:ic + 1], EBC[:, slot:slot + 1],
                                                4096.0, float(rc * 128), op0=ALU.mult, op1=ALU.add)
                        idxmap[("w2", slot, rc)] = ic
                        ic += 1
                nc.vector.drain()
                for slot in range(2):
                    for dc in range(ND):
                        for half in range(2):
                            c = idxmap[("w1", slot, dc, half)]
                            nc.vector.tensor_tensor(IDXF[:, c:c + 1], IDXF[:, c:c + 1],
                                                    IOTC[:, 1:2], op=ALU.add)
                for slot in range(2):
                    for rc in range(NFC):
                        c = idxmap[("w2", slot, rc)]
                        nc.vector.tensor_tensor(IDXF[:, c:c + 1], IDXF[:, c:c + 1],
                                                IOTC[:, 0:1], op=ALU.add)
                inst = nc.vector.nop()
                nc.vector.drain()
                env.ve.tick(nc.vector.tensor_copy(IOTC[:, 0:1], IOTC[:, 0:1]))
                nc.gpsimd.wait_ge(env.ve.sem, env.ve.n)
                for c in range(ic):
                    nc.gpsimd.tensor_copy(IDX[:, c:c + 1], IDXF[:, c:c + 1])
                inst = nc.gpsimd.engine_nop() if hasattr(nc.gpsimd, "engine_nop") else nc.gpsimd.nop()
                env.ge.tick(inst)

                w1view = dram["w0f1"].rearrange("r (h c) -> (r h) c", c=2048)
                gb = [GT0, GT1]
                gi = 0
                ve_scale = env.ve.n
                for slot in range(2):
                    s_w1 = dram["s_w1a"] if slot == 0 else dram["s_w1b"]
                    for dc in range(ND):
                        for half in range(2):
                            g = gb[gi % 2]
                            c = idxmap[("w1", slot, dc, half)]
                            nc.gpsimd.indirect_dma_start(
                                g[:, :], None, w1view,
                                bass.IndirectOffsetOnAxis(ap=IDX[:, c:c + 1], axis=0)
                            ).then_inc(env.qg.sem, 16)
                            env.qg.n += 16
                            nc.gpsimd.wait_ge(env.qg.sem, env.qg.n)
                            env.ld_g(s_w1[dc * 128:(dc + 1) * 128, half * 2048:(half + 1) * 2048], g[:, :])
                            nc.gpsimd.wait_ge(env.qg.sem, env.qg.n)
                            gi += 1
                for slot in range(2):
                    s_w2 = dram["s_w2a"] if slot == 0 else dram["s_w2b"]
                    for rc in range(NFC):
                        g = gb[gi % 2]
                        c = idxmap[("w2", slot, rc)]
                        nc.gpsimd.indirect_dma_start(
                            g[:, 0:1024], None, dram["w0f2"],
                            bass.IndirectOffsetOnAxis(ap=IDX[:, c:c + 1], axis=0)
                        ).then_inc(env.qg.sem, 16)
                        env.qg.n += 16
                        nc.vector.wait_ge(env.qg.sem, env.qg.n)
                        inst = nc.vector.tensor_scalar(g[:, 0:1024], g[:, 0:1024],
                                                       WEC[:, slot:slot + 1], None, op0=ALU.mult)
                        env.ve.tick(inst)
                        nc.gpsimd.wait_ge(env.ve.sem, env.ve.n)
                        env.ld_g(s_w2[rc * 128:(rc + 1) * 128, :], g[:, 0:1024])
                        nc.gpsimd.wait_ge(env.qg.sem, env.qg.n)
                        gi += 1
                env.barrier()

        with ExitStack() as sQ:
            XLNQ = sQ.enter_context(nc.sbuf_tensor(n("f_xlnq"), [128, ND * TQ], BF16))
            G = sQ.enter_context(nc.sbuf_tensor(n("f_g"), [128, (2 if not l1f else 1) * NFC * TQ], BF16))
            W1B = sQ.enter_context(nc.sbuf_tensor(n("f_w1b"), [128, 2 * NSL * ND * 128], BF16))
            W2B = sQ.enter_context(nc.sbuf_tensor(n("f_w2b"), [128, 2 * (2 if not l1f else 1) * NFC * 128], BF16))
            HP = sQ.enter_context(nc.psum_tensor(n("f_hp"), [128, (4 if not l1f else 2) * TQ], F32))
            YP = sQ.enter_context(nc.psum_tensor(n("f_yp"), [128, 2 * TQ], F32))
            if l1f:
                MBSB = sQ.enter_context(nc.sbuf_tensor(n("f_mbsb"), [128, TQ], BF16))
                YF = sQ.enter_context(nc.sbuf_tensor(n("f_yf"), [128, ND * TQ], F32))
            else:
                YPS = sQ.enter_context(nc.sbuf_tensor(n("f_yps"), [128, 2 * TQ], F32))

            if l1f:
                w1src = [dram["w1f1"].rearrange("e d f -> (e d) f")[e * D:(e + 1) * D, :]
                         for e in range(E)]
                w2src = [dram["w1fo"]]
            else:
                w1src = [dram["s_w1a"], dram["s_w1b"]]
                w2src = [dram["s_w2a"], dram["s_w2b"]]
            NW2 = len(w2src)

            def load_w1(fc, buf):
                base = buf * NSL * ND * 128
                for sl in range(NSL):
                    dst = W1B[:, base + sl * ND * 128: base + (sl + 1) * ND * 128]
                    dst3 = dst.rearrange("p (dc f) -> p dc f", f=128)
                    src3 = w1src[sl].rearrange("(dc p) f -> p dc f", p=128)[:, :, fc * 128:(fc + 1) * 128]
                    if l1f:
                        inst = nc.gpsimd.dma_start(dst3, src3)
                        env.qg.tick(inst, 16)
                    else:
                        inst = nc.scalar.dma_start(dst3, src3)
                        env.qa.tick(inst, 16)
                return env.qg.n if l1f else env.qa.n

            def load_w2(dc, buf):
                base = buf * NW2 * NFC * 128
                for sl in range(NW2):
                    dst = W2B[:, base + sl * NFC * 128: base + (sl + 1) * NFC * 128]
                    dst3 = dst.rearrange("p (fc d) -> p fc d", d=128)
                    src3 = w2src[sl].rearrange("(fc p) d -> p fc d", p=128)[:, :, dc * 128:(dc + 1) * 128]
                    if l1f:
                        inst = nc.gpsimd.dma_start(dst3, src3)
                        env.qg.tick(inst, 16)
                    else:
                        inst = nc.sync.dma_start(dst3, src3)
                        env.qs.tick(inst, 16)
                return env.qg.n if l1f else env.qs.n

            env.fence_dma(nc.scalar)
            env.fence_dma(nc.sync)
            for q in range(NQ):
                qscope = ExitStack()
                env.fence_dma(nc.gpsimd)
                for dc in range(ND):
                    env.ld_g(XLNQ[:, dc * TQ:(dc + 1) * TQ],
                             dram["s_xln"][dc * 128:(dc + 1) * 128, q * TQ:(q + 1) * TQ])
                xq_loaded = env.qg.n

                if l1f:
                    XM = qscope.enter_context(nc.sbuf_tensor(n("f_xm"), [128, E * ND * TQ], BF16))
                    nc.tensor.wait_ge(env.qg.sem, xq_loaded)
                    nc.tensor.wait_ge(env.se.sem, env.se.n)
                    for e in range(E):
                        inst = nc.tensor.matmul(YP[:, 0:TQ], sb["E8"][:, e * 128:(e + 1) * 128],
                                                MASKTP[:, q * TQ:(q + 1) * TQ], start=True, stop=True)
                        env.te.tick(inst)
                        nc.scalar.wait_ge(env.te.sem, env.te.n)
                        inst = nc.scalar.copy(MBSB[:, :], YP[:, 0:TQ])
                        env.se.tick(inst)
                        nc.vector.wait_ge(env.se.sem, env.se.n)
                        for dc in range(ND):
                            inst = nc.vector.tensor_tensor(
                                XM[:, (e * ND + dc) * TQ:(e * ND + dc + 1) * TQ],
                                XLNQ[:, dc * TQ:(dc + 1) * TQ], MBSB[:, :], op=ALU.mult)
                        env.ve.tick(inst)
                        nc.tensor.wait_ge(env.ve.sem, env.ve.n)

                se_g_counts = {}
                te_h_counts = {}
                ld_counts = {}
                ld_counts[0] = load_w1(0, 0)
                for fc in range(NFC):
                    if fc + 1 < NFC:
                        if fc >= 1:
                            eng = nc.gpsimd if l1f else nc.scalar
                            eng.wait_ge(env.te.sem, te_h_counts.get(fc - 1, 0))
                        ld_counts[fc + 1] = load_w1(fc + 1, (fc + 1) % 2)
                    if l1f:
                        nc.tensor.wait_ge(env.qg.sem, ld_counts[fc])
                    else:
                        nc.tensor.wait_ge(env.qa.sem, ld_counts[fc])
                        nc.tensor.wait_ge(env.qg.sem, xq_loaded)
                    base = (fc % 2) * NSL * ND * 128
                    if l1f:
                        hb = HP[:, (fc % 2) * TQ:(fc % 2 + 1) * TQ]
                        nc.tensor.wait_ge(env.se.sem, se_g_counts.get(fc - 2, 0))
                        for sl in range(NSL):
                            for dc in range(ND):
                                nc.tensor.matmul(hb,
                                                 W1B[:, base + (sl * ND + dc) * 128: base + (sl * ND + dc + 1) * 128],
                                                 XM[:, (sl * ND + dc) * TQ:(sl * ND + dc + 1) * TQ],
                                                 start=(sl == 0 and dc == 0), stop=False)
                        inst = nc.tensor.matmul(hb, B1SB[:, fc * 128:(fc + 1) * 128],
                                                MASKTP[:, q * TQ:(q + 1) * TQ],
                                                start=False, stop=True)
                        env.te.tick(inst)
                        te_h_counts[fc] = env.te.n
                        nc.scalar.wait_ge(env.te.sem, env.te.n)
                        inst = nc.scalar.activation(G[:, fc * TQ:(fc + 1) * TQ], hb, AF.Gelu,
                                                    bias=sb["CSC"][:, 2:3])
                        env.se.tick(inst)
                        se_g_counts[fc] = env.se.n
                    else:
                        nc.tensor.wait_ge(env.se.sem, se_g_counts.get(fc - 2, 0))
                        for sl in range(2):
                            hb = HP[:, ((fc % 2) * 2 + sl) * TQ:((fc % 2) * 2 + sl + 1) * TQ]
                            for dc in range(ND):
                                inst = nc.tensor.matmul(
                                    hb,
                                    W1B[:, base + (sl * ND + dc) * 128: base + (sl * ND + dc + 1) * 128],
                                    XLNQ[:, dc * TQ:(dc + 1) * TQ],
                                    start=(dc == 0), stop=(dc == ND - 1))
                            env.te.tick(inst)
                            te_h_counts[fc] = env.te.n
                            nc.scalar.wait_ge(env.te.sem, env.te.n)
                            inst = nc.scalar.activation(
                                G[:, (sl * NFC + fc) * TQ:(sl * NFC + fc + 1) * TQ], hb, AF.Gelu,
                                bias=B1C[:, sl * NFC + fc: sl * NFC + fc + 1])
                            env.se.tick(inst)
                        se_g_counts[fc] = env.se.n
                g_done_se = env.se.n

                yld = {}
                yld[0] = load_w2(0, 0)
                te_y_counts = {}
                se_y_counts = {}
                for dc in range(ND):
                    if dc + 1 < ND:
                        if dc >= 1:
                            eng2 = nc.gpsimd if l1f else nc.sync
                            eng2.wait_ge(env.te.sem, te_y_counts.get(dc - 1, 0))
                        yld[dc + 1] = load_w2(dc + 1, (dc + 1) % 2)
                    nc.tensor.wait_ge(env.qg.sem if l1f else env.qs.sem, yld[dc])
                    nc.tensor.wait_ge(env.se.sem, max(g_done_se, se_y_counts.get(dc - 2, 0)))
                    yb = YP[:, (dc % 2) * TQ:(dc % 2 + 1) * TQ]
                    base = (dc % 2) * NW2 * NFC * 128
                    first = True
                    for sl in range(NW2):
                        for fc in range(NFC):
                            gsl = G[:, (sl * NFC + fc) * TQ:(sl * NFC + fc + 1) * TQ]
                            inst = nc.tensor.matmul(
                                yb,
                                W2B[:, base + (sl * NFC + fc) * 128: base + (sl * NFC + fc + 1) * 128],
                                gsl, start=first, stop=(sl == NW2 - 1 and fc == NFC - 1))
                            first = False
                    env.te.tick(inst)
                    te_y_counts[dc] = env.te.n
                    nc.scalar.wait_ge(env.te.sem, env.te.n)
                    if l1f:
                        inst = nc.scalar.activation(YF[:, dc * TQ:(dc + 1) * TQ], yb, AF.Identity,
                                                    bias=BOC[:, dc:dc + 1])
                        env.se.tick(inst)
                        se_y_counts[dc] = env.se.n
                        nc.vector.wait_ge(env.se.sem, env.se.n)
                        inst = nc.vector.tensor_tensor(YF[:, dc * TQ:(dc + 1) * TQ],
                                                       YF[:, dc * TQ:(dc + 1) * TQ],
                                                       XLNQ[:, dc * TQ:(dc + 1) * TQ], op=ALU.add)
                        env.ve.tick(inst)
                        nc.tensor.wait_ge(env.ve.sem, env.ve.n)
                    else:
                        if dc >= 2:
                            nc.scalar.wait_ge(env.qg.sem, env.qg.n)
                        inst = nc.scalar.activation(YPS[:, (dc % 2) * TQ:(dc % 2 + 1) * TQ], yb,
                                                    AF.Identity, bias=B2C[:, dc:dc + 1])
                        env.se.tick(inst)
                        se_y_counts[dc] = env.se.n
                        nc.gpsimd.wait_ge(env.se.sem, env.se.n)
                        nc.gpsimd.wait_ge(env.qs.sem, rs_stored)
                        env.ld_g(rs_dst[dc * 128:(dc + 1) * 128, q * TQ:(q + 1) * TQ],
                                 YPS[:, (dc % 2) * TQ:(dc % 2 + 1) * TQ], accum_op=ALU.add)

                if l1f and dbg is not None and q == int(dbg.split(":")[1]) and dbg.startswith("dbgf:"):
                    qscope.close()
                    nc.gpsimd.wait_ge(env.ve.sem, env.ve.n)
                    nc.gpsimd.wait_ge(env.se.sem, env.se.n)
                    for dc in range(ND):
                        env.ld_g(dram["outT"][dc * 128:(dc + 1) * 128, 0:TQ], YF[:, dc * TQ:(dc + 1) * TQ])
                    for fc in range(3):
                        env.ld_g(dram["outT"][0:128, TQ + fc * TQ: TQ + (fc + 1) * TQ], G[:, fc * TQ:(fc + 1) * TQ])
                    nc.sync.wait_ge(env.qg.sem, env.qg.n)
                    env.barrier()
                    return
                if l1f:
                    nc.gpsimd.wait_ge(env.te.sem, env.te.n)
                    qscope.close()
                    ln2scope = ExitStack()
                    ln2 = _alloc_ln(env, ln2scope, want_out=True)
                    _emit_ln(env, ln2,
                             lambda dc: YF[:, dc * TQ:(dc + 1) * TQ],
                             None, eps_col=3,
                             gamma=lambda dc: GAC[:, dc:dc + 1],
                             beta=lambda dc: BEC[:, dc:dc + 1],
                             out_f32=lambda dc: ln2["XSQ"].bitcast(F32)[:, dc * TQ // 2:(dc + 1) * TQ // 2]
                             if False else ln2["XSQ2"][:, dc * TQ:(dc + 1) * TQ])
                    nc.gpsimd.wait_ge(env.ve.sem, env.ve.n)
                    nc.gpsimd.wait_ge(env.qs.sem, rs_stored)
                    for dc in range(ND):
                        env.ld_g(rs_dst[dc * 128:(dc + 1) * 128, q * TQ:(q + 1) * TQ],
                                 ln2["XSQ2"][:, dc * TQ:(dc + 1) * TQ], accum_op=ALU.add)
                    nc.vector.wait_ge(env.qg.sem, env.qg.n)
                    nc.tensor.wait_ge(env.qg.sem, env.qg.n)
                    nc.gpsimd.wait_ge(env.ve.sem, env.ve.n)
                    if dbg is not None and dbg.startswith("dbgx:") and q == int(dbg.split(":")[1]):
                        for dc in range(ND):
                            env.ld_g(dram["outT"][dc * 128:(dc + 1) * 128, TQ:2 * TQ],
                                     ln2["XSQ2"][:, dc * TQ:(dc + 1) * TQ])
                        nc.sync.wait_ge(env.qg.sem, env.qg.n)
                        env.barrier()
                        ln2scope.close()
                        return
                    ln2scope.close()
                else:
                    qscope.close()
                env.barrier()

            env.barrier()


_SHAPES = {}


def _declare_dram(nc):
    dram = {}
    for k, shp in _SHAPES.items():
        dram[k] = nc.dram_tensor(k, list(shp), F32, kind="ExternalInput").ap()
    dram["xT"] = nc.dram_tensor("xT", [D, T], F32, kind="ExternalInput").ap()
    dram["rsd"] = nc.dram_tensor("rsd", [D, T], F32, kind="Internal").ap()
    dram["s_w1a"] = nc.dram_tensor("s_w1a", [D, FF], BF16, kind="Internal").ap()
    dram["s_w1b"] = nc.dram_tensor("s_w1b", [D, FF], BF16, kind="Internal").ap()
    dram["s_w2a"] = nc.dram_tensor("s_w2a", [FF, D], BF16, kind="Internal").ap()
    dram["s_w2b"] = nc.dram_tensor("s_w2b", [FF, D], BF16, kind="Internal").ap()
    dram["s_w1x"] = nc.dram_tensor("s_w1x", [E * D, FF], BF16, kind="Internal").ap()
    dram["s_wo"] = nc.dram_tensor("s_wo", [FF, D], BF16, kind="Internal").ap()
    dram["s_xln"] = nc.dram_tensor("s_xln", [D, T], BF16, kind="Internal").ap()
    dram["outT"] = nc.dram_tensor("outT", [D, T], F32, kind="ExternalOutput").ap()
    return dram


def _build_program(blocks):
    nc = bass.Bass("TRN2", target_bir_lowering=False)
    with ExitStack() as ctx:
        env = Env(nc, ctx)
        env.dram = _declare_dram(nc)
        block = ctx.enter_context(nc.Block())
        _load_consts(env)
        cur = "xT"
        for b in blocks:
            if b == "dump_xln":
                with ExitStack() as sA:
                    RS = sA.enter_context(nc.sbuf_tensor("d_rs", [128, ND * T], F32))
                    env.fence_dma(nc.sync)
                    for dc in range(ND):
                        env.ld_s(RS[:, dc * T:(dc + 1) * T], env.dram[cur][dc * 128:(dc + 1) * 128, :])
                    rs_loaded = env.qs.n
                    ln_sc = _alloc_ln(env, sA)
                    nc.gpsimd.wait_ge(env.qs.sem, rs_loaded)
                    nc.gpsimd.wait_ge(env.qg.sem, env.consts_qg)
                    nc.tensor.wait_ge(env.qs.sem, env.consts_qs)
                    nc.tensor.wait_ge(env.qg.sem, env.consts_qg)
                    for q in range(NQ):
                        _emit_ln(env, ln_sc,
                                 lambda dc, q=q: RS[:, dc * T + q * TQ: dc * T + (q + 1) * TQ],
                                 lambda dc, q=q: env.sb["XLN"][:, dc * T + q * TQ: dc * T + (q + 1) * TQ])
                    nc.gpsimd.wait_ge(env.ve.sem, env.ve.n)
                    for dc in range(ND):
                        env.ld_g(env.dram["outT"][dc * 128:(dc + 1) * 128, :],
                                 env.sb["XLN"][:, dc * T:(dc + 1) * T])
                    nc.sync.wait_ge(env.qg.sem, env.qg.n)
                nc = nc
                cur = "outT"
                continue
            if b.startswith("dbg0a"):
                _emit_attn(env, "w0q", "w0k", "w0v", "w0p", "b0p", rope=False,
                           rs_src=env.dram[cur], rs_dst=env.dram["rsd"], dbg=b)
                cur = "outT"
            elif b == "l0a":
                _emit_attn(env, "w0q", "w0k", "w0v", "w0p", "b0p", rope=False,
                           rs_src=env.dram[cur], rs_dst=env.dram["rsd"])
                cur = "rsd"
            elif b == "l1a":
                _emit_attn(env, "w1q", "w1k", "w1v", "w1p", "b1p", rope=True,
                           rs_src=env.dram[cur], rs_dst=env.dram["rsd"])
                cur = "rsd"
            elif b == "l0f":
                _emit_ffn(env, False, rs_src=env.dram[cur], rs_dst=env.dram["rsd"])
                cur = "rsd"
            elif b == "l1f":
                _emit_ffn(env, True, rs_src=env.dram[cur], rs_dst=env.dram["outT"])
                cur = "outT"
            elif b.startswith("dbgf") or b.startswith("dbgx"):
                _emit_ffn(env, True, rs_src=env.dram[cur], rs_dst=env.dram["outT"], dbg=b)
                cur = "outT"
            else:
                raise NotImplementedError(b)
        env.fence_dma(nc.sync)
        if cur != "outT":
            env.ld_s(env.dram["outT"][:, :], env.dram[cur][:, :])
        nc.sync.wait_ge(env.qs.sem, env.qs.n)
    return nc


def _run_chain(inputs, blocks=ALL_BLOCKS, n_cores=None):
    global _SHAPES
    shared, xs = _host_prep(inputs)
    if n_cores is not None:
        xs = xs[:n_cores]
    _SHAPES = {k: v.shape for k, v in shared.items()}
    nc = _build_program(tuple(blocks))
    in_maps = []
    for b in range(len(xs)):
        m = dict(shared)
        m["xT"] = xs[b]
        in_maps.append(m)
    res = run_bass_kernel_spmd(nc, in_maps, core_ids=list(range(len(xs))))
    return np.stack([np.ascontiguousarray(r["outT"].T) for r in res.results])


def kernel(**inputs):
    return _run_chain(inputs, ALL_BLOCKS)
